# version 14
# baseline (speedup 1.0000x reference)
import numpy as np
import ml_dtypes
import concourse.bass as bass
import concourse.mybir as mybir
from concourse.bass_utils import run_bass_kernel_spmd
from contextlib import ExitStack

BF = ml_dtypes.bfloat16
F32 = mybir.dt.float32
BF16 = mybir.dt.bfloat16
I32 = mybir.dt.int32
OP = mybir.AluOpType
AF = mybir.ActivationFunctionType
AX = mybir.AxisListType

B, S, D = 4, 4096, 2048
HALF = 2048
NH, NKV, HD = 16, 4, 128
GQ = NH // NKV
IFF = 6464
NIC = 51
ICW = [128] * 50 + [64]
NFC = 16
SC_SCALE = float(HD) ** -0.5
EPS = 1e-5
RND = 384.5
ASLAB = 256
NASLAB = S // ASLAB
FSLAB = 512
NFSLAB = HALF // FSLAB
ENGS = ("sync", "gpsimd", "vector", "tensor", "scalar")
BUILD_PHASES = 5
DEBUG_OUTS = False
H_LIMIT = 16
H_SUB = 6
DBG_SL = 3


class Em:

    def __init__(self, nc, sems):
        self.nc = nc
        self.sems = sems
        self.cnt = {n: 0 for n in sems}
        self.waited = {}
        self.streams = {e: [] for e in ENGS}

    def wait(self, ename, sname, val=None):
        v = self.cnt[sname] if val is None else val
        if v <= 0:
            return
        if self.waited.get((ename, sname), -1) >= v:
            return
        self.waited[(ename, sname)] = v
        h = self.sems[sname]
        self.streams[ename].append(lambda eng, h=h, v=v: eng.wait_ge(h, v))

    def op(self, ename, fn, inc=None, k=1):
        if inc is None:
            self.streams[ename].append(fn)
            return None
        self.cnt[inc] += k
        h = self.sems[inc]
        self.streams[ename].append(lambda eng, fn=fn, h=h, k=k: fn(eng).then_inc(h, k))
        return self.cnt[inc]

    def dma(self, ename, out, in_, inc, **kw):
        if inc in ("LD", "LG"):
            self.wait(ename, inc)
        return self.op(ename,
                       lambda eng, o=out, i=in_, kw=dict(kw): eng.dma_start(out=o, in_=i, **kw),
                       inc=inc, k=16)

    def replay(self):
        self.wait("sync", "LD")
        self.wait("gpsimd", "LG")
        for s in self.cnt:
            if s not in ("LD", "LG", "TE", "VE", "SC", "GP"):
                self.wait("sync", s)
        with self.nc.Block() as blk:
            for ename in ENGS:
                sl = list(self.streams[ename])
                if not sl:
                    continue

                def make(fl):
                    def _(eng):
                        for f in fl:
                            f(eng)
                    return _
                getattr(blk, ename)(make(sl))
        self.streams = {e: [] for e in ENGS}


def build():
    nc = bass.Bass("TRN2", target_bir_lowering=False,
                   detect_race_conditions=False)

    xT = nc.dram_tensor("xT", [D, S], F32, kind="ExternalInput")
    xoT = nc.dram_tensor("xoT", [D, HALF], F32, kind="ExternalInput")
    x_tok = nc.dram_tensor("x_tok", [HALF, D], F32, kind="ExternalInput")
    wq = nc.dram_tensor("wq", [D, D], F32, kind="ExternalInput")
    wk = nc.dram_tensor("wk", [D, NKV * HD], F32, kind="ExternalInput")
    wv = nc.dram_tensor("wv", [D, NKV * HD], F32, kind="ExternalInput")
    wo = nc.dram_tensor("wo", [D, D], F32, kind="ExternalInput")
    wgT = nc.dram_tensor("wgT", [D, IFF], F32, kind="ExternalInput")
    wuT = nc.dram_tensor("wuT", [D, IFF], F32, kind="ExternalInput")
    wdT = nc.dram_tensor("wdT", [IFF, D], F32, kind="ExternalInput")
    cosK = nc.dram_tensor("cosK", [128, S], F32, kind="ExternalInput")
    sinK = nc.dram_tensor("sinK", [128, S], F32, kind="ExternalInput")
    cosQ = nc.dram_tensor("cosQ", [128, HALF], F32, kind="ExternalInput")
    sinQ = nc.dram_tensor("sinQ", [128, HALF], F32, kind="ExternalInput")
    ident16 = nc.dram_tensor("ident16", [128, 128], BF16, kind="ExternalInput")
    ident32 = nc.dram_tensor("ident32", [128, 128], F32, kind="ExternalInput")
    ones_in = nc.dram_tensor("ones_in", [128, 1], BF16, kind="ExternalInput")
    nb_in = nc.dram_tensor("nb_in", [128, 2], F32, kind="ExternalInput")
    onesk_in = nc.dram_tensor("onesk_in", [64, 128], F32, kind="ExternalInput")
    z64_in = nc.dram_tensor("z64_in", [64, 8], F32, kind="ExternalInput")
    y = nc.dram_tensor("y", [HALF, D], F32, kind="ExternalOutput")

    kw = dict(kind="ExternalOutput") if DEBUG_OUTS else {}
    xoT16_d = nc.dram_tensor("xoT16_d", [D, HALF], BF16, **kw)
    ao_d = nc.dram_tensor("ao_d", [D, HALF], BF16, **kw)
    x2_d = nc.dram_tensor("x2_d", [HALF, D], F32, **kw)
    x2q_d = nc.dram_tensor("x2q_d", [HALF, D], BF16, **kw)
    dq1_d = nc.dram_tensor("dq1_d", [1, HALF], F32, **kw)
    wgq_d = nc.dram_tensor("wgq_d", [NIC * 128, NFC * 128], BF16, **kw)
    wuq_d = nc.dram_tensor("wuq_d", [NIC * 128, NFC * 128], BF16, **kw)
    wdq_d = nc.dram_tensor("wdq_d", [NFC * 128, NIC * 128], BF16, **kw)
    dn_d = nc.dram_tensor("dn_d", [HALF, D], F32, **kw)
    stage_d = nc.dram_tensor("stage_d", [8, 2048], F32)
    if DEBUG_OUTS:
        hq_d = nc.dram_tensor("hq_d", [128, NIC * FSLAB], BF16, kind="ExternalOutput")
        shs_d = nc.dram_tensor("shs_d", [128, FSLAB], F32, kind="ExternalOutput")
        x2qf_d = nc.dram_tensor("x2qf_d", [128, NFC * FSLAB], BF16,
                                kind="ExternalOutput")
        g1_d = nc.dram_tensor("g1_d", [128, FSLAB], F32, kind="ExternalOutput")
        u1_d = nc.dram_tensor("u1_d", [128, FSLAB], F32, kind="ExternalOutput")
        si_d = nc.dram_tensor("si_d", [128, FSLAB], BF16, kind="ExternalOutput")
        hmx_d = nc.dram_tensor("hmx_d", [128, FSLAB], BF16, kind="ExternalOutput")
        hraw_d = nc.dram_tensor("hraw_d", [128, NIC * FSLAB], BF16,
                                kind="ExternalOutput")
        amax8_d = nc.dram_tensor("amax8_d", [128, 4], F32, kind="ExternalOutput")
        dqh8_d = nc.dram_tensor("dqh8_d", [128, 4], F32, kind="ExternalOutput")
        hmxc_d = nc.dram_tensor("hmxc_d", [128, FSLAB], BF16, kind="ExternalOutput")

    es = ExitStack()

    def sb(name, shape, dt):
        return es.enter_context(nc.sbuf_tensor(name, shape, dt))

    id16_sb = sb("id16_sb", [128, 128], BF16)
    id32_sb = sb("id32_sb", [128, 128], F32)
    ones_bf = sb("ones_bf", [128, 1], BF16)
    swg = sb("swg", [1, 8], F32)
    swg_bc = sb("swg_bc", [128, 8], F32)
    t11 = sb("t11", [1, 24], F32)
    nb_p = sb("nb_p", [128, 2], F32)
    onesk_sb = sb("onesk_sb", [64, 128], F32)
    z64_sb = sb("z64_sb", [64, 8], F32)

    sems = {n: es.enter_context(nc.semaphore(n)) for n in
            ("LD", "LG", "TE", "VE", "SC", "GP",
             "WA", "WB", "WC", "WD", "XA", "XB", "DA", "DB",
             "PA", "PB", "QA", "QB", "CW", "CX")}
    E = Em(nc, sems)

    pes = ExitStack()
    wstage2 = [pes.enter_context(nc.sbuf_tensor(f"wstage{i}", [128, IFF], F32))
               for i in range(2)]
    acc_p = pes.enter_context(nc.sbuf_tensor("acc_p", [128, 2], F32))
    red_p = pes.enter_context(nc.sbuf_tensor("red_p", [128, 2], F32))
    rl_p = pes.enter_context(nc.sbuf_tensor("rl_p", [1, 256], F32))
    seg_p = pes.enter_context(nc.sbuf_tensor("seg_p", [128, 128], F32))
    segb_p = pes.enter_context(nc.sbuf_tensor("segb_p", [128, 128], F32))
    segf_p = pes.enter_context(nc.sbuf_tensor("segf_p", [128, 128], F32))
    segi_p = pes.enter_context(nc.sbuf_tensor("segi_p", [128, 128], I32))
    fsg_p = pes.enter_context(nc.sbuf_tensor("fsg_p", [128, 128], F32))
    fsg2_p = pes.enter_context(nc.sbuf_tensor("fsg2_p", [128, 128], F32))
    rlf_p = pes.enter_context(nc.sbuf_tensor("rlf_p", [1, 128], F32))
    sA_2 = [pes.enter_context(nc.sbuf_tensor(f"sA_p{i}", [128, IFF], BF16))
            for i in range(2)]
    w2_2 = [pes.enter_context(nc.sbuf_tensor(f"w2_p{i}", [128, IFF], BF16))
            for i in range(2)]

    E.dma("sync", id16_sb[:], ident16[:], "LD")
    E.dma("sync", id32_sb[:], ident32[:], "LD")
    E.dma("sync", ones_bf[:], ones_in[:], "LD")
    E.dma("sync", nb_p[:], nb_in[:], "LD")
    E.dma("sync", onesk_sb[:], onesk_in[:], "LD")
    E.dma("sync", z64_sb[:], z64_in[:], "LD")
    ld0 = E.cnt["LD"]

    mats = [(wgT, wgq_d, NFC, IFF, 0, float(D * IFF)),
            (wuT, wuq_d, NFC, IFF, 1, float(D * IFF)),
            (wdT, wdq_d, NIC, D, 2, float(D * IFF))]

    def _vdrain(inc=None):
        if inc is not None:
            return E.op("vector", lambda v: v.drain(), inc=inc)
        E.op("vector", lambda v: v.drain())

    def _floor_block(dst_ap, src_ap, int_ap, r_ap, s_ap):
        ve = _vdrain(inc="VE")
        E.wait("gpsimd", "VE", ve)
        E.op("gpsimd", lambda gp: gp.tensor_copy(int_ap, src_ap))
        E.op("gpsimd", lambda gp: gp.drain())
        E.op("gpsimd", lambda gp: gp.tensor_copy(dst_ap, int_ap))
        gpv = E.op("gpsimd", lambda gp: gp.drain(), inc="GP")
        E.wait("vector", "GP", gpv)
        E.op("vector", lambda v: v.tensor_tensor(r_ap, src_ap, dst_ap,
                                                 OP.subtract))
        ve = _vdrain(inc="VE")
        E.wait("scalar", "VE", ve)
        E.op("scalar", lambda s: s.activation(s_ap, r_ap, AF.Sign))
        scv = E.op("scalar", lambda s: s.drain(), inc="SC")
        E.wait("vector", "SC", scv)
        E.op("vector", lambda v: v.tensor_scalar(s_ap, s_ap, 0.0, None, OP.min))
        _vdrain()
        E.op("vector", lambda v: v.tensor_tensor(dst_ap, dst_ap, s_ap, OP.add))
        _vdrain()

    def _t(i):
        return t11[0:1, i:i + 1]

    def _ts(di, si, imm0, imm1, op0, op1=None):
        if op1 is None:
            E.op("vector", lambda v: v.tensor_scalar(_t(di), _t(si), imm0,
                                                     imm1, op0))
        else:
            E.op("vector", lambda v: v.tensor_scalar(_t(di), _t(si), imm0,
                                                     imm1, op0, op1))
        _vdrain()

    def _tt(di, ai, bi, op):
        E.op("vector", lambda v: v.tensor_tensor(_t(di), _t(ai), _t(bi), op))
        _vdrain()

    N_IMM = float(D * IFF)
    R_IMM = float(np.float32(1.0 / (D * IFF)))

    for (src, dst, nch, wdt, mi, count) in mats:
        nseg = (wdt + 127) // 128
        ve_chunk = []
        for ch in range(nch):
            p0 = ch * 128
            pn = min(128, src.shape[0] - p0)
            ws = wstage2[ch % 2]
            psem = "PA" if ch % 2 == 0 else "PB"
            if ch >= 2:
                E.wait("sync", "VE", ve_chunk[ch - 2])
            else:
                E.wait("sync", "SC", E.cnt["SC"])
                E.wait("sync", "VE", E.cnt["VE"])
            ld = E.dma("sync", ws[0:pn, 0:wdt], src[p0:p0 + pn, :], psem)
            E.wait("vector", psem, ld)
            for sg in range(nseg):
                c0 = sg * 128
                cw = min(128, wdt - c0)
                E.op("vector", lambda v, ws=ws, pn=pn, sg=sg, c0=c0, cw=cw: v.reduce_sum(
                    seg_p[0:pn, sg:sg + 1], ws[0:pn, c0:c0 + cw], axis=AX.X,
                    apply_absolute_value=True))
            _vdrain()
            E.op("vector", lambda v, pn=pn, ns=nseg: v.tensor_scalar(
                segb_p[0:pn, 0:ns], seg_p[0:pn, 0:ns], 32.0, None, OP.mult))
            _floor_block(segf_p[0:pn, 0:nseg], segb_p[0:pn, 0:nseg],
                         segi_p[0:pn, 0:nseg], fsg_p[0:pn, 0:nseg],
                         fsg2_p[0:pn, 0:nseg])
            E.op("vector", lambda v, pn=pn, ns=nseg: v.tensor_tensor(
                segb_p[0:pn, 0:ns], segb_p[0:pn, 0:ns], segf_p[0:pn, 0:ns],
                OP.subtract))
            _vdrain()
            E.op("vector", lambda v, pn=pn, ns=nseg: v.reduce_sum(
                red_p[0:pn, 0:1], segf_p[0:pn, 0:ns], axis=AX.X))
            E.op("vector", lambda v, pn=pn, ns=nseg: v.reduce_sum(
                red_p[0:pn, 1:2], segb_p[0:pn, 0:ns], axis=AX.X))
            _vdrain()
            if ch == 0:
                E.op("vector", lambda v: v.tensor_copy(acc_p[:], red_p[:]))
            else:
                E.op("vector", lambda v, pn=pn: v.tensor_add(
                    acc_p[0:pn, 0:2], acc_p[0:pn, 0:2], red_p[0:pn, 0:2]))
            ve_chunk.append(_vdrain(inc="VE"))
        E.wait("gpsimd", "VE", ve_chunk[-1])
        E.dma("gpsimd", bass.AP(stage_d, 0, [[1, 128], [1, 1]]), acc_p[:, 0:1],
              "LG", allow_slow_non_contiguous=True)
        E.dma("gpsimd", bass.AP(stage_d, 128, [[1, 128], [1, 1]]), acc_p[:, 1:2],
              "LG", allow_slow_non_contiguous=True)
        lg = E.dma("gpsimd", rl_p[0:1, 0:256],
                   bass.AP(stage_d, 0, [[0, 1], [1, 256]]), "LG")
        E.wait("vector", "LG", lg)
        E.op("vector", lambda v: v.reduce_sum(t11[0:1, 0:1], rl_p[0:1, 0:128],
                                              axis=AX.X))
        _vdrain()
        _floor_block(rlf_p[0:1, 0:128], rl_p[0:1, 128:256],
                     segi_p[0:1, 0:128], fsg_p[0:1, 0:128], fsg2_p[0:1, 0:128])
        E.op("vector", lambda v: v.tensor_tensor(
            rl_p[0:1, 128:256], rl_p[0:1, 128:256], rlf_p[0:1, 0:128],
            OP.subtract))
        _vdrain()
        E.op("vector", lambda v: v.reduce_sum(t11[0:1, 1:2], rlf_p[0:1, 0:128],
                                              axis=AX.X))
        E.op("vector", lambda v: v.reduce_sum(t11[0:1, 2:3], rl_p[0:1, 128:256],
                                              axis=AX.X))
        _vdrain()
        _tt(3, 0, 1, OP.add)
        _tt(3, 3, 2, OP.add)
        _ts(3, 3, 1.0 / 32.0, None, OP.mult)
        _ts(4, 3, R_IMM, None, OP.mult)
        _ts(5, 4, 4097.0, None, OP.mult)
        _tt(6, 5, 4, OP.subtract)
        _tt(6, 5, 6, OP.subtract)
        _tt(7, 4, 6, OP.subtract)
        _ts(8, 4, N_IMM, None, OP.mult)
        _ts(9, 6, N_IMM, None, OP.mult)
        _tt(9, 9, 8, OP.subtract)
        _ts(10, 7, N_IMM, None, OP.mult)
        _tt(9, 9, 10, OP.add)
        _tt(10, 3, 8, OP.subtract)
        _tt(10, 10, 9, OP.subtract)
        _ts(10, 10, R_IMM, None, OP.mult)
        _tt(11, 4, 10, OP.add)
        _ts(11, 11, EPS, None, OP.max)
        E.op("vector", lambda v: v.reciprocal(_t(12), _t(11)))
        _vdrain()
        _ts(5, 11, 4097.0, None, OP.mult)
        _tt(13, 5, 11, OP.subtract)
        _tt(13, 5, 13, OP.subtract)
        _tt(14, 11, 13, OP.subtract)
        for _round in range(2):
            _ts(5, 12, 4097.0, None, OP.mult)
            _tt(15, 5, 12, OP.subtract)
            _tt(15, 5, 15, OP.subtract)
            _tt(16, 12, 15, OP.subtract)
            _tt(17, 12, 11, OP.mult)
            _tt(18, 15, 13, OP.mult)
            _tt(18, 18, 17, OP.subtract)
            _tt(19, 15, 14, OP.mult)
            _tt(18, 18, 19, OP.add)
            _tt(19, 16, 13, OP.mult)
            _tt(18, 18, 19, OP.add)
            _tt(19, 16, 14, OP.mult)
            _tt(18, 18, 19, OP.add)
            _ts(19, 17, -1.0, 1.0, OP.mult, OP.add)
            _tt(19, 19, 18, OP.subtract)
            _tt(19, 19, 12, OP.mult)
            _tt(12, 12, 19, OP.add)
        E.op("vector", lambda v, mi=mi: v.tensor_copy(
            swg[0:1, 2 * mi:2 * mi + 1], _t(12)))
        _vdrain()
        E.op("vector", lambda v, mi=mi: v.tensor_copy(
            swg[0:1, 2 * mi + 1:2 * mi + 2], _t(11)))
        ve_s = _vdrain(inc="VE")
        E.wait("gpsimd", "VE", ve_s)
        E.wait("gpsimd", "SC", E.cnt["SC"])
        E.dma("gpsimd", stage_d[0:1, 2 * mi:2 * mi + 2],
              swg[0:1, 2 * mi:2 * mi + 2], "LG")
        lg = E.dma("gpsimd", swg_bc[:, 2 * mi:2 * mi + 2],
                   bass.AP(stage_d, 2 * mi, [[0, 128], [1, 2]]), "LG")
        sc_chunk = []
        ve_q = []
        lg_q = []
        ve_xs = []

        def _qp_front(ch):
            p0 = ch * 128
            pn = min(128, src.shape[0] - p0)
            ws = wstage2[ch % 2]
            psem = "PA" if ch % 2 == 0 else "PB"
            if ch >= 2:
                E.wait("sync", "SC", sc_chunk[ch - 2])
            else:
                E.wait("sync", "SC", E.cnt["SC"])
                E.wait("sync", "VE", E.cnt["VE"])
            ld = E.dma("sync", ws[0:pn, 0:wdt], src[p0:p0 + pn, :], psem)
            E.wait("vector", psem, ld)
            E.wait("vector", "LG", lg)
            E.op("vector", lambda v, ws=ws, pn=pn, w=wdt, mi=mi: v.tensor_scalar(
                ws[0:pn, 0:w], ws[0:pn, 0:w],
                swg_bc[0:pn, 2 * mi:2 * mi + 1], None, OP.mult))
            ve_xs.append(E.op("vector", lambda v: v.drain(), inc="VE"))

        def _qp_back(ch):
            p0 = ch * 128
            pn = min(128, src.shape[0] - p0)
            ws = wstage2[ch % 2]
            sA_p = sA_2[ch % 2]
            w2_p = w2_2[ch % 2]
            cwsem = "CW" if ch % 2 == 0 else "CX"
            E.wait("scalar", "VE", ve_xs[ch])
            E.wait("scalar", "LG", lg)
            if ch >= 2:
                E.wait("scalar", "VE", ve_q[ch - 2])
                E.wait("scalar", cwsem, lg_q[ch - 2])
            else:
                E.wait("scalar", "VE", E.cnt["VE"])
                E.wait("scalar", "CW", E.cnt["CW"])
                E.wait("scalar", "CX", E.cnt["CX"])
            E.op("scalar", lambda s, ws=ws, sA_p=sA_p, pn=pn, w=wdt: s.activation(
                sA_p[0:pn, 0:w], ws[0:pn, 0:w], AF.Sign,
                bias=nb_p[0:pn, 0:1]))
            E.op("scalar", lambda s, ws=ws, w2_p=w2_p, pn=pn, w=wdt: s.activation(
                w2_p[0:pn, 0:w], ws[0:pn, 0:w], AF.Sign,
                bias=nb_p[0:pn, 1:2]))
            sc_chunk.append(E.op("scalar", lambda s: s.drain(), inc="SC"))
            E.wait("vector", "SC", sc_chunk[-1])
            E.op("vector", lambda v, sA_p=sA_p, pn=pn, w=wdt: v.tensor_scalar(
                sA_p[0:pn, 0:w], sA_p[0:pn, 0:w], 0.0, None, OP.max))
            E.op("vector", lambda v, w2_p=w2_p, pn=pn, w=wdt: v.tensor_scalar(
                w2_p[0:pn, 0:w], w2_p[0:pn, 0:w], 0.0, None, OP.min))
            ve_q.append(E.op("vector",
                             lambda v, sA_p=sA_p, w2_p=w2_p, pn=pn, w=wdt:
                             v.tensor_add(w2_p[0:pn, 0:w], sA_p[0:pn, 0:w],
                                          w2_p[0:pn, 0:w]), inc="VE"))
            E.wait("gpsimd", "VE", ve_q[-1])
            if mi < 2:
                E.dma("gpsimd",
                      bass.AP(dst, ch * 128,
                              [[NFC * 128, pn], [128 * NFC * 128, 50], [1, 128]]),
                      bass.AP(w2_p, 0, [[IFF, pn], [128, 50], [1, 128]]),
                      cwsem)
                lg2 = E.dma("gpsimd",
                            bass.AP(dst, 50 * 128 * NFC * 128 + ch * 128,
                                    [[NFC * 128, pn], [1, 64]]),
                            bass.AP(w2_p, 50 * 128, [[IFF, pn], [1, 64]]),
                            cwsem)
                lg_q.append(lg2)
            else:
                lg1 = E.dma("gpsimd",
                            bass.AP(dst, ch * 128,
                                    [[NIC * 128, pn],
                                     [128 * NIC * 128, NFC], [1, 128]]),
                            w2_p[0:pn, 0:D].rearrange("p (c j) -> p c j", j=128),
                            cwsem)
                lg_q.append(lg1)

        for ch in range(nch):
            _qp_front(ch)
            if ch >= 1:
                _qp_back(ch - 1)
        _qp_back(nch - 1)

    E.replay()
    pes.close()
    if BUILD_PHASES < 2:
        es.close()
        return nc

    aes = ExitStack()
    hes = ExitStack()
    K_sb = [hes.enter_context(nc.sbuf_tensor(f"K_sb{g}", [128, S], BF16))
            for g in range(NKV)]
    V_sb = [hes.enter_context(nc.sbuf_tensor(f"V_sb{g}", [128, S], BF16))
            for g in range(NKV)]
    cq_sb = hes.enter_context(nc.sbuf_tensor("cq_sb", [128, HALF], F32))
    sq_sb = hes.enter_context(nc.sbuf_tensor("sq_sb", [128, HALF], F32))
    mxk2 = hes.enter_context(nc.sbuf_tensor("mxk2", [1, NKV], F32))

    wkv32 = aes.enter_context(nc.sbuf_tensor("wkv32", [128, NFC * 512], F32))
    wk16 = aes.enter_context(nc.sbuf_tensor("wk16", [128, NFC * 512], BF16))
    wv16 = aes.enter_context(nc.sbuf_tensor("wv16", [128, NFC * 512], BF16))
    xstage = aes.enter_context(nc.sbuf_tensor("xstage", [128, NFC * ASLAB], F32))
    xc16 = [aes.enter_context(nc.sbuf_tensor(f"xc16_{i}", [128, NFC * ASLAB], BF16))
            for i in range(2)]
    ck_sb = [aes.enter_context(nc.sbuf_tensor(f"ck_sb{i}", [128, ASLAB], F32))
             for i in range(2)]
    sk_sb = [aes.enter_context(nc.sbuf_tensor(f"sk_sb{i}", [128, ASLAB], F32))
             for i in range(2)]
    kev_sb = aes.enter_context(nc.sbuf_tensor("kev_sb", [128, NKV * ASLAB], BF16))
    ksh_sb = aes.enter_context(nc.sbuf_tensor("ksh_sb", [128, NKV * ASLAB], BF16))
    rt_sb = aes.enter_context(nc.sbuf_tensor("rt_sb", [128, 2 * ASLAB], F32))
    ksq16 = aes.enter_context(nc.sbuf_tensor("ksq16", [128, S], BF16))
    psK = [aes.enter_context(nc.psum_tensor(f"psK{g}", [128, 512], F32))
           for g in range(NKV)]
    psV = [aes.enter_context(nc.psum_tensor(f"psV{i}", [128, 512], F32))
           for i in range(4)]

    E.dma("sync", cq_sb[:], cosQ[:], "LD")
    E.dma("sync", sq_sb[:], sinQ[:], "LD")
    ld = E.dma("sync", wkv32[:],
               bass.AP(wk, 0, [[NKV * HD, 128], [128 * NKV * HD, NFC], [1, 512]]),
               "LD")
    E.wait("vector", "LD", ld)
    vek = E.op("vector", lambda v: v.tensor_copy(wk16[:], wkv32[:]), inc="VE")
    E.wait("sync", "VE", vek)
    ld = E.dma("sync", wkv32[:],
               bass.AP(wv, 0, [[NKV * HD, 128], [128 * NKV * HD, NFC], [1, 512]]),
               "LD")
    E.wait("vector", "LD", ld)
    vev = E.op("vector", lambda v: v.tensor_copy(wv16[:], wkv32[:]), inc="VE")

    ve_cast = [None, None]
    te_cons = [None, None]
    ve_rope_done = []
    ve_vevac_done = []
    ld_tab = [None, None]
    for sl in range(NASLAB):
        buf = sl % 2
        s0 = sl * ASLAB
        if sl >= 2:
            E.wait("sync", "VE", ve_rope_done[sl - 2])
        E.dma("sync", ck_sb[buf][:, 0:ASLAB], cosK[:, s0:s0 + ASLAB], "LD")
        ld_tab[buf] = E.dma("sync", sk_sb[buf][:, 0:ASLAB], sinK[:, s0:s0 + ASLAB],
                            "LD")
        if sl >= 1:
            E.wait("sync", "VE", ve_cast[(sl - 1) % 2])
        ld = E.dma("sync", xstage[:],
                   bass.AP(xT, s0, [[S, 128], [128 * S, NFC], [1, ASLAB]]), "LD")
        E.wait("vector", "LD", ld)
        if te_cons[buf] is not None:
            E.wait("vector", "TE", te_cons[buf])
        ve_cast[buf] = E.op("vector", lambda v, b=buf: v.tensor_copy(
            xc16[b][:], xstage[:]), inc="VE")
        E.wait("tensor", "VE", ve_cast[buf])
        E.wait("tensor", "VE", vev)
        if sl >= 1:
            E.wait("tensor", "VE", ve_rope_done[sl - 1])
            E.wait("tensor", "VE", ve_vevac_done[sl - 1])
        te_k = []
        for g in range(NKV):
            for fc in range(NFC):
                def mk(t, g=g, fc=fc, b=buf):
                    return t.matmul(psK[g][:, 0:ASLAB],
                                    wk16[:, fc * 512 + g * 128: fc * 512 + (g + 1) * 128],
                                    xc16[b][:, fc * ASLAB:(fc + 1) * ASLAB],
                                    start=(fc == 0), stop=(fc == NFC - 1))
                if fc == NFC - 1:
                    te_k.append(E.op("tensor", mk, inc="TE"))
                else:
                    E.op("tensor", mk)
        te_v = []
        for i in range(2):
            for fc in range(NFC):
                def mv(t, i=i, fc=fc, b=buf):
                    return t.matmul(psV[i][:, 0:512],
                                    xc16[b][:, fc * ASLAB + i * 128: fc * ASLAB + (i + 1) * 128],
                                    wv16[:, fc * 512:(fc + 1) * 512],
                                    start=(fc == 0), stop=(fc == NFC - 1))
                if fc == NFC - 1:
                    te_v.append(E.op("tensor", mv, inc="TE"))
                else:
                    E.op("tensor", mv)
        te_cons[buf] = E.cnt["TE"]
        E.wait("vector", "LD", ld_tab[buf])
        for g in range(NKV):
            E.wait("vector", "TE", te_k[g])
            E.op("vector", lambda v, g=g: v.tensor_copy(
                kev_sb[:, g * ASLAB:(g + 1) * ASLAB], psK[g][:, 0:ASLAB]))
        ve_kev = E.op("vector", lambda v: v.drain(), inc="VE")
        E.wait("gpsimd", "VE", ve_kev)
        E.dma("gpsimd", ksh_sb[0:64, :], kev_sb[64:128, :], "LG")
        lg_ksh = E.dma("gpsimd", ksh_sb[64:128, :], kev_sb[0:64, :], "LG")
        E.wait("vector", "LG", lg_ksh)
        for g in range(NKV):
            c = ck_sb[buf]
            s_ = sk_sb[buf]
            gb = slice(g * ASLAB, (g + 1) * ASLAB)

            def r1(v, gb=gb, c=c):
                return v.tensor_mul(rt_sb[0:64, 0:ASLAB], kev_sb[0:64, gb],
                                    c[0:64, 0:ASLAB])
            E.op("vector", r1)

            def r2(v, gb=gb, s_=s_):
                return v.tensor_mul(rt_sb[0:64, ASLAB:2 * ASLAB],
                                    ksh_sb[0:64, gb], s_[0:64, 0:ASLAB])
            E.op("vector", r2)

            def r3(v, g=g, s0=s0):
                return v.tensor_sub(K_sb[g][0:64, s0:s0 + ASLAB],
                                    rt_sb[0:64, 0:ASLAB],
                                    rt_sb[0:64, ASLAB:2 * ASLAB])
            E.op("vector", r3)

            def r4(v, gb=gb, s_=s_):
                return v.tensor_mul(rt_sb[64:128, 0:ASLAB], ksh_sb[64:128, gb],
                                    s_[64:128, 0:ASLAB])
            E.op("vector", r4)

            def r5(v, gb=gb, c=c):
                return v.tensor_mul(rt_sb[64:128, ASLAB:2 * ASLAB],
                                    kev_sb[64:128, gb], c[64:128, 0:ASLAB])
            E.op("vector", r5)

            def r6(v, g=g, s0=s0):
                return v.tensor_add(K_sb[g][64:128, s0:s0 + ASLAB],
                                    rt_sb[64:128, 0:ASLAB],
                                    rt_sb[64:128, ASLAB:2 * ASLAB])
            if g == NKV - 1:
                ve_rope_done.append(E.op("vector", r6, inc="VE"))
            else:
                E.op("vector", r6)
        for i in range(2):
            E.wait("vector", "TE", te_v[i])
            kt = sl * 2 + i
            for g in range(NKV):
                def ev(v, i=i, g=g, kt=kt):
                    return v.tensor_copy(V_sb[g][:, kt * 128:(kt + 1) * 128],
                                         psV[i][:, g * 128:(g + 1) * 128])
                if i == 1 and g == NKV - 1:
                    ve_vevac_done.append(E.op("vector", ev, inc="VE"))
                else:
                    E.op("vector", ev)

    ve_xo = None
    lg_xo = None
    for osl in range(8):
        s0 = osl * ASLAB
        if ve_xo is not None:
            E.wait("sync", "VE", ve_xo)
        E.wait("sync", "VE", max(ve_cast[0] or 0, ve_cast[1] or 0))
        E.wait("sync", "TE", max(te_cons[0] or 0, te_cons[1] or 0))
        ld = E.dma("sync", xstage[:],
                   bass.AP(xoT, s0, [[HALF, 128], [128 * HALF, NFC], [1, ASLAB]]),
                   "LD")
        E.wait("vector", "LD", ld)
        if lg_xo is not None:
            E.wait("vector", "LG", lg_xo)
        ve_xo = E.op("vector", lambda v: v.tensor_copy(
            xc16[0][:], xstage[:]), inc="VE")
        E.wait("gpsimd", "VE", ve_xo)
        lg_xo = E.dma("gpsimd",
                      bass.AP(xoT16_d, s0,
                              [[HALF, 128], [128 * HALF, NFC], [1, ASLAB]]),
                      xc16[0][:], "LG")

    ve_n = E.cnt["VE"]
    for g in range(NKV):
        E.wait("scalar", "VE", ve_rope_done[-1])
        E.wait("scalar", "TE", E.cnt["TE"])
        sc = E.op("scalar", lambda s, g=g: s.activation(
            ksq16[:], K_sb[g][:], AF.Square), inc="SC")
        E.wait("tensor", "SC", sc)
        first = True
        for n in range(S // 512):
            pb = psK[n % 2]
            te = E.op("tensor", lambda t, g=g, n=n, pb=pb: t.matmul(
                pb[0:1, 0:512], ones_bf[:, 0:1],
                ksq16[:, n * 512:(n + 1) * 512], start=True, stop=True), inc="TE")
            E.wait("vector", "TE", te)
            E.op("vector", lambda v, pb=pb: v.reduce_max(
                t11[0:1, 2:3], pb[0:1, 0:512], axis=AX.X))
            E.op("vector", lambda v: v.drain())
            if first:
                E.op("vector", lambda v, g=g: v.tensor_copy(
                    mxk2[0:1, g:g + 1], t11[0:1, 2:3]))
                E.op("vector", lambda v: v.drain())
                first = False
            else:
                E.op("vector", lambda v, g=g: v.tensor_max(
                    mxk2[0:1, g:g + 1], mxk2[0:1, g:g + 1], t11[0:1, 2:3]))
                E.op("vector", lambda v: v.drain())
            ve = E.op("vector", lambda v: v.drain(), inc="VE")
            E.wait("tensor", "VE", ve)

    E.replay()
    aes.close()
    if BUILD_PHASES < 3:
        hes.close()
        es.close()
        return nc

    bes = ExitStack()
    wqs32 = [bes.enter_context(nc.sbuf_tensor(f"wqs32_{i}", [128, D], F32))
             for i in range(2)]
    wq16 = [bes.enter_context(nc.sbuf_tensor(f"wq16_{i}", [128, D], BF16))
            for i in range(2)]
    xot16 = [bes.enter_context(nc.sbuf_tensor(f"xot16_{i}", [128, HALF], BF16))
             for i in range(2)]
    Q_sb = bes.enter_context(nc.sbuf_tensor("Q_sb", [128, HALF], BF16))
    qev_sb = bes.enter_context(nc.sbuf_tensor("qev_sb", [128, HALF], BF16))
    qsh_sb = bes.enter_context(nc.sbuf_tensor("qsh_sb", [128, HALF], BF16))
    rt2_sb = bes.enter_context(nc.sbuf_tensor("rt2_sb", [128, 2048], F32))
    qsq16 = bes.enter_context(nc.sbuf_tensor("qsq16", [128, HALF], BF16))
    PT_sb = [bes.enter_context(nc.sbuf_tensor(f"PT_{i}", [128, 1024], BF16))
             for i in range(2)]
    bc_sb = bes.enter_context(nc.sbuf_tensor("bc_sb", [128, 1024], F32))
    rs_sb = bes.enter_context(nc.sbuf_tensor("rs_sb", [1, 1024], F32))
    ao16_sb = bes.enter_context(nc.sbuf_tensor("ao16_sb", [128, 1024], BF16))
    negC_sb = bes.enter_context(nc.sbuf_tensor("negC_sb", [128, 1], F32))
    ps_sc = [bes.enter_context(nc.psum_tensor(f"ps_sc{i}", [128, 1024], F32))
             for i in range(2)]
    ps_pv = bes.enter_context(nc.psum_tensor("ps_pv", [128, 1024], F32))
    ps_sum = bes.enter_context(nc.psum_tensor("ps_sum", [128, 1024], F32))

    ld_wq = [None, None]
    ve_wq16 = [None, None]
    te_q_prev = [0, 0]
    te_fc_done = [0, 0]
    sc_exp_last2 = [0, 0]
    te_pv_last = 0
    ve_ao_last = 0
    ve_recip_last = 0
    lg_ao_prev = 0
    lg_negc = 0
    lg_qsh = 0

    for h in range(min(NH, H_LIMIT)):
        g = h // GQ
        hb = h % 2
        if ve_wq16[hb] is not None:
            E.wait("sync", "VE", ve_wq16[hb])
        qsem = "QA" if hb == 0 else "QB"
        ld_wq[hb] = E.dma("sync", wqs32[hb][:],
                          bass.AP(wq, h * 128, [[D, 128], [128 * D, NFC], [1, 128]]),
                          qsem)
        E.wait("vector", qsem, ld_wq[hb])
        E.wait("vector", "TE", te_q_prev[hb])
        ve_wq16[hb] = E.op("vector", lambda v, hb=hb: v.tensor_copy(
            wq16[hb][:], wqs32[hb][:]), inc="VE")

        te_q = None
        ld_xot = [None, None]
        for fc in range(NFC):
            xb = fc % 2
            E.wait("sync", "TE", te_fc_done[xb])
            xsem = "XA" if xb == 0 else "XB"
            ld_xot[xb] = E.dma("sync", xot16[xb][:],
                               xoT16_d[fc * 128:(fc + 1) * 128, :], xsem)
            E.wait("tensor", xsem, ld_xot[xb])
            E.wait("tensor", "VE", ve_wq16[hb])
            if fc == 0:
                E.wait("tensor", "SC", sc_exp_last2[0])
                E.wait("tensor", "SC", sc_exp_last2[1])
                E.wait("tensor", "VE", ve_recip_last)
            for cc in range(4):
                pb = ps_sc[cc // 2]
                off = (cc % 2) * 512

                def mq(t, fc=fc, cc=cc, pb=pb, off=off, hb=hb, xb=xb):
                    return t.matmul(pb[:, off:off + 512],
                                    wq16[hb][:, fc * 128:(fc + 1) * 128],
                                    xot16[xb][:, cc * 512:(cc + 1) * 512],
                                    start=(fc == 0), stop=(fc == NFC - 1))
                if cc == 3:
                    te_fc_done[xb] = E.op("tensor", mq, inc="TE")
                    if fc == NFC - 1:
                        te_q = te_fc_done[xb]
                else:
                    E.op("tensor", mq)
        te_q_prev[hb] = te_q
        if H_SUB < 2:
            continue
        E.wait("vector", "TE", te_q)
        E.wait("vector", "LG", lg_qsh)
        for cc in range(2):
            E.op("vector", lambda v, cc=cc: v.tensor_copy(
                qev_sb[:, cc * 1024:(cc + 1) * 1024], ps_sc[cc][:]))
        ve_qev = E.op("vector", lambda v: v.drain(), inc="VE")
        E.wait("gpsimd", "VE", ve_qev)
        E.dma("gpsimd", qsh_sb[0:64, :], qev_sb[64:128, :], "LG")
        lg_qsh = E.dma("gpsimd", qsh_sb[64:128, :], qev_sb[0:64, :], "LG")
        E.wait("vector", "LG", lg_qsh)
        for cc in range(2):
            q0 = cc * 1024
            qs = slice(q0, q0 + 1024)

            def q1(v, qs=qs):
                return v.tensor_mul(rt2_sb[0:64, 0:1024], qev_sb[0:64, qs],
                                    cq_sb[0:64, qs])
            E.op("vector", q1)

            def q2(v, qs=qs):
                return v.tensor_mul(rt2_sb[0:64, 1024:2048], qsh_sb[0:64, qs],
                                    sq_sb[0:64, qs])
            E.op("vector", q2)

            def q3(v, qs=qs):
                return v.tensor_sub(Q_sb[0:64, qs], rt2_sb[0:64, 0:1024],
                                    rt2_sb[0:64, 1024:2048])
            E.op("vector", q3)

            def q4(v, qs=qs):
                return v.tensor_mul(rt2_sb[64:128, 0:1024], qsh_sb[64:128, qs],
                                    sq_sb[64:128, qs])
            E.op("vector", q4)

            def q5(v, qs=qs):
                return v.tensor_mul(rt2_sb[64:128, 1024:2048], qev_sb[64:128, qs],
                                    cq_sb[64:128, qs])
            E.op("vector", q5)

            def q6(v, qs=qs):
                return v.tensor_add(Q_sb[64:128, qs], rt2_sb[64:128, 0:1024],
                                    rt2_sb[64:128, 1024:2048])
            E.op("vector", q6)
        ve_rope = E.op("vector", lambda v: v.drain(), inc="VE")

        if H_SUB < 3:
            continue
        E.wait("scalar", "VE", ve_rope)
        sc_sq = E.op("scalar", lambda s: s.activation(qsq16[:], Q_sb[:], AF.Square),
                     inc="SC")
        E.wait("tensor", "SC", sc_sq)
        for j in range(4):
            off = (j % 2) * 512
            te = E.op("tensor", lambda t, j=j, off=off: t.matmul(
                ps_sum[0:1, off:off + 512], ones_bf[:, 0:1],
                qsq16[:, j * 512:(j + 1) * 512], start=True, stop=True), inc="TE")
            E.wait("vector", "TE", te)
            E.op("vector", lambda v, off=off: v.reduce_max(
                t11[0:1, 3:4], ps_sum[0:1, off:off + 512], axis=AX.X))
            E.op("vector", lambda v: v.drain())
            if j == 0:
                E.op("vector", lambda v: v.tensor_copy(t11[0:1, 4:5], t11[0:1, 3:4]))
                E.op("vector", lambda v: v.drain())
            else:
                E.op("vector", lambda v: v.tensor_max(
                    t11[0:1, 4:5], t11[0:1, 4:5], t11[0:1, 3:4]))
                E.op("vector", lambda v: v.drain())
            ve = E.op("vector", lambda v: v.drain(), inc="VE")
            E.wait("tensor", "VE", ve)
        E.op("vector", lambda v, g=g: v.tensor_add(
            t11[0:1, 5:6], t11[0:1, 4:5], mxk2[0:1, g:g + 1]))
        E.op("vector", lambda v: v.drain())
        ve_c = E.op("vector", lambda v: v.tensor_scalar(
            t11[0:1, 6:7], t11[0:1, 5:6], -0.5 * SC_SCALE, None, OP.mult), inc="VE")
        E.op("vector", lambda v: v.drain())
        if H_SUB < 4:
            continue
        E.op("vector", lambda v: v.tensor_copy(z64_sb[0:1, 0:1], t11[0:1, 6:7]))
        ve_z = E.op("vector", lambda v: v.drain(), inc="VE")
        E.wait("tensor", "VE", ve_z)
        te_nc = E.op("tensor", lambda t: t.matmul(
            ps_sum[0:128, 0:1], onesk_sb[:], z64_sb[0:64, 0:1],
            start=True, stop=True), inc="TE")
        E.wait("vector", "TE", te_nc)
        E.wait("vector", "SC", max(sc_exp_last2))
        lg_negc = 0
        ve_negc = E.op("vector", lambda v: v.tensor_copy(negC_sb[:],
                                                         ps_sum[0:128, 0:1]),
                       inc="VE")

        if H_SUB < 5:
            continue
        for qc in range(2):
            q0 = qc * 1024
            te_s = [None] * 32
            sc_e = [None] * 32
            te_p = [None] * 32

            def emit_scores(t_idx):
                buf = t_idx % 2
                if t_idx >= 2:
                    E.wait("tensor", "SC", sc_e[t_idx - 2])
                elif qc == 0:
                    pass
                else:
                    E.wait("tensor", "SC", prev_exp[buf])
                for half in range(2):
                    def ms(t, t_idx=t_idx, half=half, buf=buf, g=g, q0=q0):
                        return t.matmul(
                            ps_sc[buf][:, half * 512:(half + 1) * 512],
                            K_sb[g][:, t_idx * 128:(t_idx + 1) * 128],
                            Q_sb[:, q0 + half * 512: q0 + (half + 1) * 512],
                            start=True, stop=True)
                    if half == 1:
                        te_s[t_idx] = E.op("tensor", ms, inc="TE")
                    else:
                        E.op("tensor", ms)

            prev_exp = [sc_exp_last2[0], sc_exp_last2[1]]
            emit_scores(0)
            emit_scores(1)
            for t_idx in range(32):
                buf = t_idx % 2
                E.wait("scalar", "TE", te_s[t_idx])
                E.wait("scalar", "VE", ve_negc)
                if t_idx >= 2:
                    E.wait("scalar", "TE", te_p[t_idx - 2])
                elif te_pv_last:
                    E.wait("scalar", "TE", te_pv_last)

                def me(s, buf=buf):
                    return s.activation(PT_sb[buf][:], ps_sc[buf][:], AF.Exp,
                                        bias=negC_sb[:, 0:1], scale=SC_SCALE)
                sc_e[t_idx] = E.op("scalar", me, inc="SC")
                sc_exp_last2[buf] = sc_e[t_idx]
                if t_idx + 2 < 32:
                    emit_scores(t_idx + 2)
                E.wait("tensor", "SC", sc_e[t_idx])
                if t_idx == 0:
                    E.wait("tensor", "VE", ve_ao_last)
                    E.wait("tensor", "VE", ve_recip_last)
                for half in range(2):
                    def mp(t, t_idx=t_idx, half=half, buf=buf, g=g):
                        return t.matmul(
                            ps_pv[:, half * 512:(half + 1) * 512],
                            V_sb[g][:, t_idx * 128:(t_idx + 1) * 128],
                            PT_sb[buf][:, half * 512:(half + 1) * 512],
                            start=(t_idx == 0), stop=(t_idx == 31))
                    E.op("tensor", mp)
                for half in range(2):
                    def mu(t, t_idx=t_idx, half=half, buf=buf):
                        return t.matmul(
                            ps_sum[0:1, half * 512:(half + 1) * 512],
                            ones_bf[:, 0:1],
                            PT_sb[buf][:, half * 512:(half + 1) * 512],
                            start=(t_idx == 0), stop=(t_idx == 31))
                    if half == 1:
                        te_p[t_idx] = E.op("tensor", mu, inc="TE")
                    else:
                        E.op("tensor", mu)
            te_pv_last = te_p[31]
            if H_SUB < 6:
                continue
            E.wait("vector", "TE", te_pv_last)
            ve_recip_last = E.op("vector", lambda v: v.reciprocal(
                rs_sb[:], ps_sum[0:1, :]), inc="VE")
            E.wait("gpsimd", "VE", ve_recip_last)
            E.dma("gpsimd", stage_d[2:3, 0:1024], rs_sb[:], "LG")
            lg_bc = E.dma("gpsimd", bc_sb[:],
                          bass.AP(stage_d, 2 * 2048, [[0, 128], [1, 1024]]), "LG")
            E.wait("vector", "LG", lg_bc)
            E.wait("vector", "LG", lg_ao_prev)
            ve_ao_last = E.op("vector", lambda v: v.tensor_mul(
                ao16_sb[:], ps_pv[:], bc_sb[:]), inc="VE")
            E.wait("gpsimd", "VE", ve_ao_last)
            lg_ao_prev = E.dma("gpsimd",
                               ao_d[h * 128:(h + 1) * 128, q0:q0 + 1024],
                               ao16_sb[:], "LG")

    E.replay()
    bes.close()
    hes.close()
    if BUILD_PHASES < 4:
        es.close()
        return nc

    wes = ExitStack()
    wo16 = wes.enter_context(nc.sbuf_tensor("wo16", [128, NFC * D], BF16))
    wof32 = wes.enter_context(nc.sbuf_tensor("wof32", [128, D], F32))
    aop = [wes.enter_context(nc.sbuf_tensor(f"aop{i}", [128, NFC * 128], BF16))
           for i in range(2)]
    xt_sb = [wes.enter_context(nc.sbuf_tensor(f"xt_sb{i}", [128, D], F32))
             for i in range(2)]
    res1_sb = wes.enter_context(nc.sbuf_tensor("res1_sb", [128, D], F32))
    x2_sb = wes.enter_context(nc.sbuf_tensor("x2_sb", [128, D], F32))
    v_sb = wes.enter_context(nc.sbuf_tensor("v_sb", [128, D], F32))
    vi_sb = wes.enter_context(nc.sbuf_tensor("vi_sb", [128, D], I32))
    vr_sb = wes.enter_context(nc.sbuf_tensor("vr_sb", [128, D], F32))
    x2q_sb = wes.enter_context(nc.sbuf_tensor("x2q_sb", [128, D], BF16))
    d_sb = wes.enter_context(nc.sbuf_tensor("d_sb", [128, D], F32))
    dsq16 = wes.enter_context(nc.sbuf_tensor("dsq16", [128, D], BF16))
    lnt = wes.enter_context(nc.sbuf_tensor("lnt", [128, 8], F32))
    ps_wo = [wes.enter_context(nc.psum_tensor(f"ps_wo{i}", [128, 2048], F32))
             for i in range(2)]

    ve_woc = None
    for fc in range(NFC):
        if ve_woc is not None:
            E.wait("sync", "VE", ve_woc)
        ld = E.dma("sync", wof32[:], wo[fc * 128:(fc + 1) * 128, :], "LD")
        E.wait("vector", "LD", ld)
        ve_woc = E.op("vector", lambda v, fc=fc: v.tensor_copy(
            wo16[:, fc * D:(fc + 1) * D], wof32[:]), inc="VE")

    ve_tt_done = [0, 0]
    te_wo_done = [0, 0]
    lg_tt_done = [0, 0]
    for tt in range(NFC):
        tb = tt % 2
        r0 = tt * 128
        E.wait("sync", "VE", ve_tt_done[tb])
        E.wait("sync", "TE", te_wo_done[tb])
        wsem = "WA" if tb == 0 else "WB"
        ld_ao = E.dma("sync", aop[tb][:],
                      bass.AP(ao_d, r0, [[HALF, 128], [128 * HALF, NFC], [1, 128]]),
                      wsem)
        ld_xt = E.dma("sync", xt_sb[tb][:], x_tok[r0:r0 + 128, :], wsem)
        E.wait("tensor", wsem, ld_ao)
        E.wait("tensor", "VE", ve_woc)
        if tt >= 2:
            E.wait("tensor", "VE", ve_tt_done[tb])
        te_wo = None
        for c in range(NFC):
            for n in range(4):
                def mw(t, c=c, n=n, tb=tb):
                    return t.matmul(ps_wo[tb][:, n * 512:(n + 1) * 512],
                                    aop[tb][:, c * 128:(c + 1) * 128],
                                    wo16[:, c * D + n * 512: c * D + (n + 1) * 512],
                                    start=(c == 0), stop=(c == NFC - 1))
                if c == NFC - 1 and n == 3:
                    te_wo = E.op("tensor", mw, inc="TE")
                else:
                    E.op("tensor", mw)
        te_wo_done[tb] = te_wo
        E.wait("vector", "TE", te_wo)
        E.wait("vector", wsem, ld_xt)
        E.wait("vector", "LG", lg_tt_done[1 - tb])
        E.op("vector", lambda v, tb=tb: v.scalar_tensor_tensor(
            res1_sb[:], ps_wo[tb][:], 1.0, xt_sb[tb][:], OP.mult, OP.add))
        E.op("vector", lambda v: v.reduce_sum(lnt[:, 0:1], res1_sb[:], axis=AX.X))
        E.op("vector", lambda v: v.drain())
        E.op("vector", lambda v: v.tensor_scalar(lnt[:, 0:1], lnt[:, 0:1],
                                                 1.0 / D, None, OP.mult))
        E.op("vector", lambda v: v.drain())
        ve_d = E.op("vector", lambda v: v.tensor_scalar(
            d_sb[:], res1_sb[:], lnt[:, 0:1], None, OP.subtract), inc="VE")
        E.wait("scalar", "VE", ve_d)
        sc_sq = E.op("scalar", lambda s: s.activation(dsq16[:], d_sb[:],
                                                      AF.Square), inc="SC")
        E.wait("vector", "SC", sc_sq)
        E.op("vector", lambda v: v.reduce_sum(lnt[:, 3:4], dsq16[:], axis=AX.X))
        E.op("vector", lambda v: v.drain())
        E.op("vector", lambda v: v.tensor_scalar(
            lnt[:, 3:4], lnt[:, 3:4], 1.0 / D, EPS, OP.mult, OP.add))
        E.op("vector", lambda v: v.drain())
        ve_r = E.op("vector", lambda v: v.reciprocal(lnt[:, 4:5], lnt[:, 3:4]),
                    inc="VE")
        E.wait("scalar", "VE", ve_r)
        sc_rs = E.op("scalar", lambda s: s.activation(lnt[:, 5:6], lnt[:, 4:5],
                                                      AF.Sqrt), inc="SC")
        E.wait("vector", "SC", sc_rs)
        E.op("vector", lambda v: v.tensor_scalar(
            x2_sb[:], d_sb[:], lnt[:, 5:6], None, OP.mult))
        E.op("vector", lambda v: v.reduce_max(lnt[:, 6:7], x2_sb[:], axis=AX.X,
                                              apply_absolute_value=True))
        E.op("vector", lambda v: v.drain())
        E.op("vector", lambda v: v.tensor_scalar(lnt[:, 6:7], lnt[:, 6:7], EPS,
                                                 None, OP.max))
        E.op("vector", lambda v: v.drain())
        E.op("vector", lambda v: v.reciprocal(lnt[:, 7:8], lnt[:, 6:7]))
        E.op("vector", lambda v: v.drain())
        E.op("vector", lambda v: v.tensor_scalar(lnt[:, 7:8], lnt[:, 7:8], 127.0,
                                                 None, OP.mult))
        E.op("vector", lambda v: v.drain())
        E.op("vector", lambda v: v.tensor_scalar(lnt[:, 6:7], lnt[:, 6:7],
                                                 1.0 / 127.0, None, OP.mult))
        E.op("vector", lambda v: v.drain())
        ve_v = E.op("vector", lambda v: v.tensor_scalar(
            v_sb[:], x2_sb[:], lnt[:, 7:8], RND, OP.mult, OP.add), inc="VE")
        E.wait("gpsimd", "VE", ve_v)
        E.op("gpsimd", lambda gp: gp.tensor_copy(vi_sb[:], v_sb[:]))
        E.op("gpsimd", lambda gp: gp.drain())
        E.op("gpsimd", lambda gp: gp.tensor_copy(vr_sb[:], vi_sb[:]))
        gp_r = E.op("gpsimd", lambda gp: gp.drain(), inc="GP")
        E.wait("vector", "GP", gp_r)
        E.op("vector", lambda v: v.tensor_tensor(v_sb[:], v_sb[:], vr_sb[:],
                                                 OP.subtract))
        ve_rr = E.op("vector", lambda v: v.drain(), inc="VE")
        E.wait("scalar", "VE", ve_rr)
        E.op("scalar", lambda s: s.activation(dsq16[:], v_sb[:], AF.Sign))
        sc_sg = E.op("scalar", lambda s: s.drain(), inc="SC")
        E.wait("vector", "SC", sc_sg)
        E.op("vector", lambda v: v.tensor_scalar(dsq16[:], dsq16[:], 0.0, None,
                                                 OP.min))
        E.op("vector", lambda v: v.tensor_tensor(v_sb[:], vr_sb[:], dsq16[:],
                                                 OP.add))
        ve_q = E.op("vector", lambda v: v.tensor_scalar(
            x2q_sb[:], v_sb[:], -384.0, 127.0, OP.add, OP.min), inc="VE")
        E.wait("gpsimd", "VE", ve_q)
        E.dma("gpsimd", x2_d[r0:r0 + 128, :], x2_sb[:], "LG")
        E.dma("gpsimd", x2q_d[r0:r0 + 128, :], x2q_sb[:], "LG")
        lg_tt_done[tb] = E.dma("gpsimd", dq1_d[0:1, r0:r0 + 128], lnt[:, 6:7],
                               "LG", allow_slow_non_contiguous=True)
        ve_tt_done[tb] = ve_q

    E.replay()
    wes.close()
    if BUILD_PHASES < 5:
        es.close()
        return nc

    fes = ExitStack()
    h_sb = fes.enter_context(nc.sbuf_tensor("h_sb", [128, NIC * FSLAB], BF16))
    x2qf = fes.enter_context(nc.sbuf_tensor("x2qf", [128, NFC * FSLAB], BF16))
    x2qrow = [fes.enter_context(nc.sbuf_tensor(f"x2qrow{i}", [128, D], BF16))
              for i in range(2)]
    wgt = [fes.enter_context(nc.sbuf_tensor(f"wgt{i}", [128, NFC * 128], BF16))
           for i in range(2)]
    wut = [fes.enter_context(nc.sbuf_tensor(f"wut{i}", [128, NFC * 128], BF16))
           for i in range(2)]
    wdt_sb = [fes.enter_context(nc.sbuf_tensor(f"wdt{i}", [128, NIC * 128], BF16))
              for i in range(2)]
    g1_sb = fes.enter_context(nc.sbuf_tensor("g1_sb", [128, FSLAB], F32))
    u1_sb = fes.enter_context(nc.sbuf_tensor("u1_sb", [128, FSLAB], F32))
    silu_sb = fes.enter_context(nc.sbuf_tensor("silu_sb", [128, FSLAB], BF16))
    hmx_sb = fes.enter_context(nc.sbuf_tensor("hmx_sb", [128, FSLAB], BF16))
    hmn_sb = fes.enter_context(nc.sbuf_tensor("hmn_sb", [128, FSLAB], BF16))
    QG = 2
    vf_sb = fes.enter_context(nc.sbuf_tensor("vf_sb", [128, QG * FSLAB], F32))
    vfi_sb = fes.enter_context(nc.sbuf_tensor("vfi_sb", [128, QG * FSLAB], I32))
    vfr_sb = fes.enter_context(nc.sbuf_tensor("vfr_sb", [128, QG * FSLAB], F32))
    qsg_sb = fes.enter_context(nc.sbuf_tensor("qsg_sb", [128, QG * FSLAB], BF16))
    shs_rp = fes.enter_context(nc.sbuf_tensor("shs_rp", [128, QG * FSLAB], F32))
    gsc_bc = fes.enter_context(nc.sbuf_tensor("gsc_bc", [128, FSLAB], F32))
    usc_bc = fes.enter_context(nc.sbuf_tensor("usc_bc", [128, FSLAB], F32))
    shs_bc = fes.enter_context(nc.sbuf_tensor("shs_bc", [128, FSLAB], F32))
    row_sb = fes.enter_context(nc.sbuf_tensor("row_sb", [1, FSLAB], F32))
    amax8 = fes.enter_context(nc.sbuf_tensor("amax8", [128, 4], F32))
    dqh8 = fes.enter_context(nc.sbuf_tensor("dqh8", [128, 4], F32))
    dnf_sb = fes.enter_context(nc.sbuf_tensor("dnf_sb", [128, FSLAB], F32))
    dnT_sb = fes.enter_context(nc.sbuf_tensor("dnT_sb", [128, 4 * 128], F32))
    dnrow = [fes.enter_context(nc.sbuf_tensor(f"dnrow{i}", [128, D], F32))
             for i in range(2)]
    x2row = [fes.enter_context(nc.sbuf_tensor(f"x2row{i}", [128, D], F32))
             for i in range(2)]
    yout_sb = fes.enter_context(nc.sbuf_tensor("yout_sb", [128, D], F32))
    dsqf16 = fes.enter_context(nc.sbuf_tensor("dsqf16", [128, D], BF16))
    lntf = fes.enter_context(nc.sbuf_tensor("lntf", [128, 8], F32))
    dcomb = fes.enter_context(nc.sbuf_tensor("dcomb", [128, 1], F32))
    ps_g = fes.enter_context(nc.psum_tensor("ps_g", [128, 512], F32))
    ps_u = fes.enter_context(nc.psum_tensor("ps_u", [128, 512], F32))
    ps_d = fes.enter_context(nc.psum_tensor("ps_d", [128, 512], F32))
    ps_tp = [fes.enter_context(nc.psum_tensor(f"ps_tp{i}", [128, 1024], BF16))
             for i in range(2)]
    ps_tf = fes.enter_context(nc.psum_tensor("ps_tf", [128, 512], F32))

    NT8 = FSLAB // 128
    y_st = []
    te_x2q = [0, 0]
    te_gu_last = 0
    te_dn_last = 0
    ve_ln2 = [0, 0]
    for sl in range(NFSLAB):
        s0 = sl * FSLAB
        ve_tpev = E.cnt["VE"]
        for t8 in range(NT8):
            xb = t8 % 2
            E.wait("sync", "TE", te_x2q[xb])
            xsem = "XA" if xb == 0 else "XB"
            ld = E.dma("sync", x2qrow[xb][:],
                       x2q_d[s0 + t8 * 128: s0 + (t8 + 1) * 128, :], xsem)
            E.wait("tensor", xsem, ld)
            if sl > 0:
                E.wait("tensor", "TE", te_gu_last)
            for fc in range(NFC):
                pb = ps_tp[fc % 2]
                te = E.op("tensor", lambda t, fc=fc, xb=xb, pb=pb: t.transpose(
                    pb[:, 0:128], x2qrow[xb][:, fc * 128:(fc + 1) * 128],
                    id16_sb[:]), inc="TE")
                te_x2q[xb] = te
                E.wait("vector", "TE", te)
                ve_tpev = E.op("vector", lambda v, fc=fc, t8=t8, pb=pb: v.tensor_copy(
                    x2qf[:, fc * FSLAB + t8 * 128: fc * FSLAB + (t8 + 1) * 128],
                    pb[:, 0:128]), inc="VE")
                E.wait("tensor", "VE", ve_tpev)
        E.wait("sync", "VE", E.cnt["VE"])
        E.wait("sync", "LG", E.cnt["LG"])
        ld = E.dma("sync", row_sb[:], dq1_d[0:1, s0:s0 + FSLAB], "LD")
        E.wait("vector", "LD", ld)
        E.op("vector", lambda v: v.tensor_scalar(
            row_sb[:], row_sb[:], swg[0:1, 1:2], None, OP.mult))
        E.op("vector", lambda v: v.drain())
        ve = E.op("vector", lambda v: v.drain(), inc="VE")
        E.wait("gpsimd", "VE", ve)
        E.dma("gpsimd", stage_d[3:4, 0:FSLAB], row_sb[:], "LG")
        lg = E.dma("gpsimd", gsc_bc[:],
                   bass.AP(stage_d, 3 * 2048, [[0, 128], [1, FSLAB]]), "LG")
        E.wait("sync", "VE", ve)
        E.wait("sync", "LG", lg)
        ld = E.dma("sync", row_sb[:], dq1_d[0:1, s0:s0 + FSLAB], "LD")
        E.wait("vector", "LD", ld)
        E.wait("vector", "LG", lg)
        E.op("vector", lambda v: v.tensor_scalar(
            row_sb[:], row_sb[:], swg[0:1, 3:4], None, OP.mult))
        E.op("vector", lambda v: v.drain())
        ve = E.op("vector", lambda v: v.drain(), inc="VE")
        E.wait("gpsimd", "VE", ve)
        E.dma("gpsimd", stage_d[3:4, 0:FSLAB], row_sb[:], "LG")
        lg_usc = E.dma("gpsimd", usc_bc[:],
                       bass.AP(stage_d, 3 * 2048, [[0, 128], [1, FSLAB]]), "LG")

        ve_g1 = 0
        ve_u1 = 0
        ve_h = 0
        ld_w = [0, 0]
        for ic in range(NIC):
            wb = ic % 2
            pn = ICW[ic]
            if ic >= 2:
                E.wait("sync", "TE", te_gu[wb])
            wsem = "WA" if wb == 0 else "WB"
            if pn == 128:
                E.dma("sync", wgt[wb][:, 0:NFC * 128],
                      bass.AP(wgq_d, ic * 128 * NFC * 128,
                              [[NFC * 128, 128], [1, NFC * 128]]), wsem)
                ld_w[wb] = E.dma("sync", wut[wb][:, 0:NFC * 128],
                                 bass.AP(wuq_d, ic * 128 * NFC * 128,
                                         [[NFC * 128, 128], [1, NFC * 128]]),
                                 wsem)
            else:
                E.dma("sync",
                      wgt[wb][:, 0:NFC * pn].rearrange("p (c j) -> p c j", j=pn),
                      bass.AP(wgq_d, ic * 128 * NFC * 128,
                              [[NFC * 128, 128], [128, NFC], [1, pn]]), wsem)
                ld_w[wb] = E.dma("sync",
                                 wut[wb][:, 0:NFC * pn].rearrange(
                                     "p (c j) -> p c j", j=pn),
                                 bass.AP(wuq_d, ic * 128 * NFC * 128,
                                         [[NFC * 128, 128], [128, NFC], [1, pn]]),
                                 wsem)
            E.wait("tensor", wsem, ld_w[wb])
            E.wait("tensor", "VE", ve_g1)
            if ic == 0:
                E.wait("tensor", "VE", ve_tpev)
            for fc in range(NFC):
                def mg(t, ic=ic, fc=fc, wb=wb, pn=pn):
                    return t.matmul(ps_g[0:pn, 0:FSLAB],
                                    wgt[wb][:, fc * pn: (fc + 1) * pn],
                                    x2qf[:, fc * FSLAB:(fc + 1) * FSLAB],
                                    start=(fc == 0), stop=(fc == NFC - 1))
                if fc == NFC - 1:
                    te_gate = E.op("tensor", mg, inc="TE")
                else:
                    E.op("tensor", mg)
            E.wait("tensor", "VE", ve_u1)
            for fc in range(NFC):
                def mu2(t, ic=ic, fc=fc, wb=wb, pn=pn):
                    return t.matmul(ps_u[0:pn, 0:FSLAB],
                                    wut[wb][:, fc * pn: (fc + 1) * pn],
                                    x2qf[:, fc * FSLAB:(fc + 1) * FSLAB],
                                    start=(fc == 0), stop=(fc == NFC - 1))
                if fc == NFC - 1:
                    te_up = E.op("tensor", mu2, inc="TE")
                else:
                    E.op("tensor", mu2)
            if ic == 0:
                te_gu = [None, None]
            te_gu[wb] = te_up
            te_gu_last = te_up
            E.wait("vector", "TE", te_gate)
            E.wait("vector", "LG", lg_usc)
            E.op("vector", lambda v, pn=pn: v.tensor_mul(
                g1_sb[0:pn, :], ps_g[0:pn, 0:FSLAB], gsc_bc[0:pn, :]))
            ve_g1 = E.op("vector", lambda v: v.drain(), inc="VE")
            E.wait("scalar", "VE", ve_g1)
            E.op("scalar", lambda s, pn=pn: s.activation(
                silu_sb[0:pn, :], g1_sb[0:pn, :], AF.Silu))
            sc_si = E.op("scalar", lambda s: s.drain(), inc="SC")
            E.wait("vector", "TE", te_up)
            E.op("vector", lambda v, pn=pn: v.tensor_mul(
                u1_sb[0:pn, :], ps_u[0:pn, 0:FSLAB], usc_bc[0:pn, :]))
            ve_u1 = E.op("vector", lambda v: v.drain(), inc="VE")
            E.wait("vector", "SC", sc_si)
            if sl > 0 and ic == 0:
                E.wait("vector", "TE", te_dn_last)
            E.op("vector", lambda v, ic=ic, pn=pn: v.tensor_mul(
                h_sb[0:pn, ic * FSLAB:(ic + 1) * FSLAB], silu_sb[0:pn, :],
                u1_sb[0:pn, :]))
            if DEBUG_OUTS and sl == DBG_SL and ic == 12:
                ve_dbg = E.op("vector", lambda v: v.drain(), inc="VE")
                E.wait("gpsimd", "VE", ve_dbg)
                E.dma("gpsimd", g1_d[:], g1_sb[:], "LG")
                E.dma("gpsimd", u1_d[:], u1_sb[:], "LG")
                E.dma("gpsimd", si_d[:], silu_sb[:], "LG")
                E.wait("vector", "LG", E.cnt["LG"])
                E.wait("scalar", "LG", E.cnt["LG"])
            if ic == 0:
                E.op("vector", lambda v: v.tensor_copy(hmx_sb[:], h_sb[:, 0:FSLAB]))
                E.op("vector", lambda v: v.tensor_copy(
                    hmn_sb[:], h_sb[:, 0:FSLAB]))
                ve_h = E.op("vector", lambda v: v.drain(), inc="VE")
            else:
                E.op("vector", lambda v, ic=ic, pn=pn: v.tensor_tensor(
                    hmx_sb[0:pn, :], hmx_sb[0:pn, :],
                    h_sb[0:pn, ic * FSLAB:(ic + 1) * FSLAB], OP.max))
                E.op("vector", lambda v, ic=ic, pn=pn: v.tensor_tensor(
                    hmn_sb[0:pn, :], hmn_sb[0:pn, :],
                    h_sb[0:pn, ic * FSLAB:(ic + 1) * FSLAB], OP.min))
                ve_h = E.op("vector", lambda v: v.drain(), inc="VE")

        if DEBUG_OUTS and sl == DBG_SL:
            ve_dbg2 = E.op("vector", lambda v: v.drain(), inc="VE")
            E.wait("gpsimd", "VE", ve_dbg2)
            E.dma("gpsimd", hraw_d[:], h_sb[:], "LG")
            E.dma("gpsimd", hmx_d[:], hmx_sb[:], "LG")
            E.wait("vector", "LG", E.cnt["LG"])
        E.wait("vector", "VE", ve_h)
        E.op("vector", lambda v: v.tensor_scalar(
            hmn_sb[:], hmn_sb[:], -1.0, None, OP.mult))
        E.op("vector", lambda v: v.tensor_tensor(
            hmx_sb[:], hmx_sb[:], hmn_sb[:], OP.max))
        ve_h = E.op("vector", lambda v: v.drain(), inc="VE")
        if DEBUG_OUTS and sl == DBG_SL:
            E.wait("gpsimd", "VE", ve_h)
            E.dma("gpsimd", hmxc_d[:], hmx_sb[:], "LG")
            E.wait("vector", "LG", E.cnt["LG"])
            E.wait("tensor", "LG", E.cnt["LG"])
        E.wait("tensor", "VE", ve_h)
        for t8 in range(NT8):
            pb = ps_tp[t8 % 2]
            te = E.op("tensor", lambda t, t8=t8, pb=pb: t.transpose(
                pb[:, 0:128], hmx_sb[:, t8 * 128:(t8 + 1) * 128], id16_sb[:]),
                inc="TE")
            E.wait("vector", "TE", te)
            E.op("vector", lambda v, t8=t8, pb=pb: v.reduce_max(
                amax8[:, t8:t8 + 1], pb[:, 0:128], axis=AX.X))
            E.op("vector", lambda v: v.drain())
            E.op("vector", lambda v, t8=t8: v.tensor_scalar(
                amax8[:, t8:t8 + 1], amax8[:, t8:t8 + 1], EPS, None, OP.max))
            E.op("vector", lambda v: v.drain())
            E.op("vector", lambda v, t8=t8: v.tensor_mul(
                dqh8[:, t8:t8 + 1], amax8[:, t8:t8 + 1], swg_bc[:, 5:6]))
            E.op("vector", lambda v: v.drain())
            E.op("vector", lambda v, t8=t8: v.tensor_scalar(
                dqh8[:, t8:t8 + 1], dqh8[:, t8:t8 + 1], 1.0 / 127.0, None,
                OP.mult))
            E.op("vector", lambda v: v.drain())
            ve = E.op("vector", lambda v: v.drain(), inc="VE")
            E.wait("tensor", "VE", ve)
            E.wait("gpsimd", "VE", ve)
            E.dma("gpsimd", stage_d[4:5, t8 * 128:(t8 + 1) * 128],
                  amax8[:, t8:t8 + 1], "LG", allow_slow_non_contiguous=True)
        E.wait("gpsimd", "VE", E.cnt["VE"])
        if DEBUG_OUTS and sl == DBG_SL:
            E.dma("gpsimd", amax8_d[:], amax8[:], "LG")
            E.dma("gpsimd", dqh8_d[:], dqh8[:], "LG")
            E.wait("vector", "LG", E.cnt["LG"])
        lg = E.dma("gpsimd", row_sb[:], stage_d[4:5, 0:FSLAB], "LG")
        E.wait("vector", "LG", lg)
        E.op("vector", lambda v: v.reciprocal(row_sb[:], row_sb[:]))
        E.op("vector", lambda v: v.drain())
        E.op("vector", lambda v: v.tensor_scalar(row_sb[:], row_sb[:], 127.0,
                                                 None, OP.mult))
        E.op("vector", lambda v: v.drain())
        ve = E.op("vector", lambda v: v.drain(), inc="VE")
        E.wait("gpsimd", "VE", ve)
        E.dma("gpsimd", stage_d[5:6, 0:FSLAB], row_sb[:], "LG")
        lg_sh = E.dma("gpsimd", shs_bc[:],
                      bass.AP(stage_d, 5 * 2048, [[0, 128], [1, FSLAB]]), "LG")
        lg_shr = E.dma("gpsimd", shs_rp[:],
                       bass.AP(stage_d, 5 * 2048,
                               [[0, 128], [0, QG], [1, FSLAB]]), "LG")

        qgroups = []
        hq_done = [0] * NIC
        _g = 0
        while _g < NIC:
            _n = min(QG, NIC - _g)
            while ICW[_g + _n - 1] != ICW[_g]:
                _n -= 1
            qgroups.append((_g, _n))
            _g += _n
        E.wait("vector", "LG", lg_shr)
        for (g0, gn) in qgroups:
            pn = ICW[g0]
            cw = gn * FSLAB
            c0 = g0 * FSLAB
            E.op("vector", lambda v, pn=pn, cw=cw, c0=c0: v.tensor_mul(
                vf_sb[0:pn, 0:cw], h_sb[0:pn, c0:c0 + cw], shs_rp[0:pn, 0:cw]))
            E.op("vector", lambda v, pn=pn, cw=cw: v.tensor_scalar(
                vf_sb[0:pn, 0:cw], vf_sb[0:pn, 0:cw], RND, None, OP.add))
            ve = E.op("vector", lambda v: v.drain(), inc="VE")
            E.wait("gpsimd", "VE", ve)
            E.op("gpsimd", lambda gp, pn=pn, cw=cw: gp.tensor_copy(
                vfi_sb[0:pn, 0:cw], vf_sb[0:pn, 0:cw]))
            E.op("gpsimd", lambda gp: gp.drain())
            E.op("gpsimd", lambda gp, pn=pn, cw=cw: gp.tensor_copy(
                vfr_sb[0:pn, 0:cw], vfi_sb[0:pn, 0:cw]))
            gp_hq = E.op("gpsimd", lambda gp: gp.drain(), inc="GP")
            E.wait("vector", "GP", gp_hq)
            E.op("vector", lambda v, pn=pn, cw=cw: v.tensor_tensor(
                vf_sb[0:pn, 0:cw], vf_sb[0:pn, 0:cw], vfr_sb[0:pn, 0:cw],
                OP.subtract))
            ve_rr = E.op("vector", lambda v: v.drain(), inc="VE")
            E.wait("scalar", "VE", ve_rr)
            E.op("scalar", lambda s, pn=pn, cw=cw: s.activation(
                qsg_sb[0:pn, 0:cw], vf_sb[0:pn, 0:cw], AF.Sign))
            sc_sg = E.op("scalar", lambda s: s.drain(), inc="SC")
            E.wait("vector", "SC", sc_sg)
            E.op("vector", lambda v, pn=pn, cw=cw: v.tensor_scalar(
                qsg_sb[0:pn, 0:cw], qsg_sb[0:pn, 0:cw], 0.0, None, OP.min))
            E.op("vector", lambda v, pn=pn, cw=cw: v.tensor_tensor(
                vf_sb[0:pn, 0:cw], vfr_sb[0:pn, 0:cw], qsg_sb[0:pn, 0:cw],
                OP.add))
            E.op("vector", lambda v, pn=pn, cw=cw, c0=c0: v.tensor_scalar(
                h_sb[0:pn, c0:c0 + cw], vf_sb[0:pn, 0:cw],
                -384.0, 127.0, OP.add, OP.min))
            m = E.op("vector", lambda v: v.drain(), inc="VE")
            for _ic in range(g0, g0 + gn):
                hq_done[_ic] = m
        ve_hq = hq_done[NIC - 1]

        if DEBUG_OUTS and sl == DBG_SL:
            E.wait("gpsimd", "VE", E.cnt["VE"])
            E.dma("gpsimd", hq_d[:], h_sb[:], "LG")
            E.dma("gpsimd", shs_d[:], shs_bc[:], "LG")
            E.dma("gpsimd", x2qf_d[:], x2qf[:], "LG")
            E.wait("vector", "LG", E.cnt["LG"])
            E.wait("tensor", "LG", E.cnt["LG"])
        lg_dn = 0
        ve_dn = 0
        for dc in range(NFC):
            wb = dc % 2
            if dc >= 2:
                E.wait("sync", "TE", te_dn[wb])
            dsem = "WC" if wb == 0 else "WD"
            E.dma("sync", wdt_sb[wb][:, 0:50 * 128],
                  bass.AP(wdq_d, dc * 128 * NIC * 128,
                          [[NIC * 128, 128], [1, 50 * 128]]), dsem)
            ld_wd = E.dma("sync", wdt_sb[wb][0:64, 50 * 128:51 * 128],
                          bass.AP(wdq_d, dc * 128 * NIC * 128 + 50 * 128,
                                  [[NIC * 128, 64], [1, 128]]), dsem)
            E.wait("tensor", dsem, ld_wd)
            E.wait("tensor", "VE", ve_dn)
            for ic in range(NIC):
                pn = ICW[ic]
                E.wait("tensor", "VE", hq_done[ic])

                def md(t, ic=ic, dc=dc, wb=wb, pn=pn):
                    return t.matmul(ps_d[:, 0:FSLAB],
                                    wdt_sb[wb][0:pn, ic * 128:(ic + 1) * 128],
                                    h_sb[0:pn, ic * FSLAB:(ic + 1) * FSLAB],
                                    start=(ic == 0), stop=(ic == NIC - 1))
                if ic == NIC - 1:
                    te_d = E.op("tensor", md, inc="TE")
                else:
                    E.op("tensor", md)
            if dc == 0:
                te_dn = [None, None]
            te_dn[wb] = te_d
            te_dn_last = te_d
            E.wait("vector", "TE", te_d)
            E.op("vector", lambda v: v.tensor_copy(dnf_sb[:], ps_d[:, 0:FSLAB]))
            ve_dn = E.op("vector", lambda v: v.drain(), inc="VE")
            E.wait("tensor", "VE", ve_dn)
            if lg_dn:
                E.wait("tensor", "LG", lg_dn)
            for t8 in range(NT8):
                te = E.op("tensor", lambda t, t8=t8: t.transpose(
                    ps_tf[:, 0:128], dnf_sb[:, t8 * 128:(t8 + 1) * 128],
                    id32_sb[:]), inc="TE")
                E.wait("vector", "TE", te)
                E.op("vector", lambda v, t8=t8: v.tensor_copy(
                    dnT_sb[:, t8 * 128:(t8 + 1) * 128], ps_tf[:, 0:128]))
                ve = E.op("vector", lambda v: v.drain(), inc="VE")
                E.wait("tensor", "VE", ve)
            E.wait("gpsimd", "VE", E.cnt["VE"])
            lg_dn = E.dma("gpsimd",
                          bass.AP(dn_d, s0 * D + dc * 128,
                                  [[D, 128], [128 * D, NT8], [1, 128]]),
                          bass.AP(dnT_sb, 0, [[NT8 * 128, 128], [128, NT8], [1, 128]]),
                          "LG")

        for t8 in range(NT8):
            xb = t8 % 2
            r0 = s0 + t8 * 128
            E.wait("sync", "LG", lg_dn)
            E.wait("sync", "VE", ve_ln2[xb])
            lsem = "DA" if xb == 0 else "DB"
            ld_dn = E.dma("sync", dnrow[xb][:], dn_d[r0:r0 + 128, :], lsem)
            ld_x2 = E.dma("sync", x2row[xb][:], x2_d[r0:r0 + 128, :], lsem)
            E.wait("vector", lsem, ld_x2)
            if y_st:
                E.wait("vector", "LD", y_st[-1])
            E.op("vector", lambda v, t8=t8: v.tensor_copy(dcomb[:],
                                                          dqh8[:, t8:t8 + 1]))
            E.op("vector", lambda v: v.drain())
            E.op("vector", lambda v, xb=xb: v.scalar_tensor_tensor(
                dnrow[xb][:], dnrow[xb][:], dcomb[:, 0:1], x2row[xb][:],
                OP.mult, OP.add))
            E.op("vector", lambda v: v.drain())
            E.op("vector", lambda v, xb=xb: v.reduce_sum(lntf[:, 0:1],
                                                         dnrow[xb][:],
                                                         axis=AX.X))
            E.op("vector", lambda v: v.drain())
            E.op("vector", lambda v: v.tensor_scalar(lntf[:, 0:1], lntf[:, 0:1],
                                                     1.0 / D, None, OP.mult))
            E.op("vector", lambda v: v.drain())
            E.op("vector", lambda v, xb=xb: v.tensor_scalar(
                x2row[xb][:], dnrow[xb][:], lntf[:, 0:1], None, OP.subtract))
            ve_d2 = E.op("vector", lambda v: v.drain(), inc="VE")
            E.wait("scalar", "VE", ve_d2)
            E.op("scalar", lambda s, xb=xb: s.activation(dsqf16[:],
                                                         x2row[xb][:],
                                                         AF.Square))
            sc_q2 = E.op("scalar", lambda s: s.drain(), inc="SC")
            E.wait("vector", "SC", sc_q2)
            E.op("vector", lambda v: v.reduce_sum(lntf[:, 3:4], dsqf16[:],
                                                  axis=AX.X))
            E.op("vector", lambda v: v.drain())
            E.op("vector", lambda v: v.tensor_scalar(
                lntf[:, 3:4], lntf[:, 3:4], 1.0 / D, EPS, OP.mult, OP.add))
            E.op("vector", lambda v: v.drain())
            E.op("vector", lambda v: v.reciprocal(lntf[:, 4:5],
                                                   lntf[:, 3:4]))
            ve_r2 = E.op("vector", lambda v: v.drain(), inc="VE")
            E.wait("scalar", "VE", ve_r2)
            E.op("scalar", lambda s: s.activation(lntf[:, 5:6],
                                                  lntf[:, 4:5], AF.Sqrt))
            sc_r2 = E.op("scalar", lambda s: s.drain(), inc="SC")
            E.wait("vector", "SC", sc_r2)
            E.op("vector", lambda v, xb=xb: v.tensor_scalar(
                yout_sb[:], x2row[xb][:], lntf[:, 5:6], None, OP.mult))
            ve_y = E.op("vector", lambda v: v.drain(), inc="VE")
            ve_ln2[xb] = ve_y
            E.wait("sync", "VE", ve_y)
            y_st.append(E.dma("sync", y[r0:r0 + 128, :], yout_sb[:], "LD"))

    E.wait("sync", "LD", E.cnt["LD"])
    E.wait("gpsimd", "LG", E.cnt["LG"])
    E.replay()
    fes.close()
    es.close()
    return nc


_PERM = np.concatenate([np.arange(0, 128, 2), np.arange(1, 128, 2)])


def _rope_tables():
    inv = (1.0 / 10000.0 ** (np.arange(0, HD, 2, dtype=np.float64) / HD)
           ).astype(np.float32)
    t = np.arange(S, dtype=np.float32)
    freqs = (t[:, None] * inv[None, :]).astype(np.float32)
    cos = np.cos(freqs.astype(np.float64)).astype(np.float32).T
    sin = np.sin(freqs.astype(np.float64)).astype(np.float32).T
    return (np.ascontiguousarray(np.vstack([cos, cos])),
            np.ascontiguousarray(np.vstack([sin, sin])))


def _permute_cols(w, nheads):
    hd = 128
    idx = np.concatenate([h * hd + _PERM for h in range(nheads)])
    return np.ascontiguousarray(w[:, idx])


def kernel(L0_attn_hidden_states, L0_attn_Wq, L0_attn_Wk, L0_attn_Wv, L0_attn_Wo,
           L0_ffn_w_gate, L0_ffn_w_up, L0_ffn_w_down):
    x = np.ascontiguousarray(np.asarray(L0_attn_hidden_states, dtype=np.float32))
    wq_h = _permute_cols(np.asarray(L0_attn_Wq, dtype=np.float32), NH)
    wk_h = _permute_cols(np.asarray(L0_attn_Wk, dtype=np.float32), NKV)
    wv_h = np.ascontiguousarray(np.asarray(L0_attn_Wv, dtype=np.float32))
    wo_h = np.ascontiguousarray(np.asarray(L0_attn_Wo, dtype=np.float32))
    wgT_h = np.ascontiguousarray(np.asarray(L0_ffn_w_gate, dtype=np.float32).T)
    wuT_h = np.ascontiguousarray(np.asarray(L0_ffn_w_up, dtype=np.float32).T)
    wdT_h = np.ascontiguousarray(np.asarray(L0_ffn_w_down, dtype=np.float32).T)
    cos, sin = _rope_tables()
    id16 = np.eye(128, dtype=BF)
    id32 = np.eye(128, dtype=np.float32)

    onesk = np.zeros((64, 128), dtype=np.float32)
    onesk[0, :] = 1.0
    nc = build()
    in_maps = []
    for c in range(8):
        b, hf = c // 2, c % 2
        xb = x[b]
        xbT = np.ascontiguousarray(xb.T)
        in_maps.append({
            "xT": xbT,
            "xoT": np.ascontiguousarray(xbT[:, hf * HALF:(hf + 1) * HALF]),
            "x_tok": np.ascontiguousarray(xb[hf * HALF:(hf + 1) * HALF, :]),
            "wq": wq_h, "wk": wk_h, "wv": wv_h, "wo": wo_h,
            "wgT": wgT_h, "wuT": wuT_h, "wdT": wdT_h,
            "cosK": cos, "sinK": sin,
            "cosQ": np.ascontiguousarray(cos[:, hf * HALF:(hf + 1) * HALF]),
            "sinQ": np.ascontiguousarray(sin[:, hf * HALF:(hf + 1) * HALF]),
            "ident16": id16, "ident32": id32,
            "ones_in": np.ones((128, 1), dtype=BF),
            "nb_in": np.broadcast_to(np.array([-0.5, 0.5], dtype=np.float32),
                                     (128, 2)).copy(),
            "onesk_in": onesk,
            "z64_in": np.zeros((64, 8), dtype=np.float32),
        })
    res = run_bass_kernel_spmd(nc, in_maps, core_ids=list(range(8)))
    out = np.empty((B, S, D), dtype=np.float32)
    for c in range(8):
        b, hf = c // 2, c % 2
        out[b, hf * HALF:(hf + 1) * HALF, :] = res.results[c]["y"]
    return out


# version 16 (final)
# speedup vs baseline: 1.0059x; 1.0059x over previous
import numpy as np
import ml_dtypes
import concourse.bass as bass
import concourse.mybir as mybir
from concourse.bass_utils import run_bass_kernel_spmd
from contextlib import ExitStack

BF = ml_dtypes.bfloat16
F32 = mybir.dt.float32
BF16 = mybir.dt.bfloat16
I32 = mybir.dt.int32
OP = mybir.AluOpType
AF = mybir.ActivationFunctionType
AX = mybir.AxisListType

B, S, D = 4, 4096, 2048
HALF = 2048
NH, NKV, HD = 16, 4, 128
GQ = NH // NKV
IFF = 6464
NIC = 51
ICW = [128] * 50 + [64]
NFC = 16
SC_SCALE = float(HD) ** -0.5
EPS = 1e-5
RND = 384.5
ASLAB = 256
NASLAB = S // ASLAB
FSLAB = 512
NFSLAB = HALF // FSLAB
ENGS = ("sync", "gpsimd", "vector", "tensor", "scalar")
BUILD_PHASES = 5
DEBUG_OUTS = False
H_LIMIT = 16
H_SUB = 6
DBG_SL = 3


class Em:

    def __init__(self, nc, sems):
        self.nc = nc
        self.sems = sems
        self.cnt = {n: 0 for n in sems}
        self.waited = {}
        self.streams = {e: [] for e in ENGS}

    def wait(self, ename, sname, val=None):
        v = self.cnt[sname] if val is None else val
        if v <= 0:
            return
        if self.waited.get((ename, sname), -1) >= v:
            return
        self.waited[(ename, sname)] = v
        h = self.sems[sname]
        self.streams[ename].append(lambda eng, h=h, v=v: eng.wait_ge(h, v))

    def op(self, ename, fn, inc=None, k=1):
        if inc is None:
            self.streams[ename].append(fn)
            return None
        self.cnt[inc] += k
        h = self.sems[inc]
        self.streams[ename].append(lambda eng, fn=fn, h=h, k=k: fn(eng).then_inc(h, k))
        return self.cnt[inc]

    def dma(self, ename, out, in_, inc, **kw):
        if inc in ("LD", "LG"):
            self.wait(ename, inc)
        return self.op(ename,
                       lambda eng, o=out, i=in_, kw=dict(kw): eng.dma_start(out=o, in_=i, **kw),
                       inc=inc, k=16)

    def replay(self):
        self.wait("sync", "LD")
        self.wait("gpsimd", "LG")
        for s in self.cnt:
            if s not in ("LD", "LG", "TE", "VE", "SC", "GP"):
                self.wait("sync", s)
        with self.nc.Block() as blk:
            for ename in ENGS:
                sl = list(self.streams[ename])
                if not sl:
                    continue

                def make(fl):
                    def _(eng):
                        for f in fl:
                            f(eng)
                    return _
                getattr(blk, ename)(make(sl))
        self.streams = {e: [] for e in ENGS}


def build():
    nc = bass.Bass("TRN2", target_bir_lowering=False,
                   detect_race_conditions=False)

    xT = nc.dram_tensor("xT", [D, S], F32, kind="ExternalInput")
    xoT = nc.dram_tensor("xoT", [D, HALF], F32, kind="ExternalInput")
    x_tok = nc.dram_tensor("x_tok", [HALF, D], F32, kind="ExternalInput")
    wq = nc.dram_tensor("wq", [D, D], F32, kind="ExternalInput")
    wk = nc.dram_tensor("wk", [D, NKV * HD], F32, kind="ExternalInput")
    wv = nc.dram_tensor("wv", [D, NKV * HD], F32, kind="ExternalInput")
    wo = nc.dram_tensor("wo", [D, D], F32, kind="ExternalInput")
    wgT = nc.dram_tensor("wgT", [D, IFF], F32, kind="ExternalInput")
    wuT = nc.dram_tensor("wuT", [D, IFF], F32, kind="ExternalInput")
    wdT = nc.dram_tensor("wdT", [IFF, D], F32, kind="ExternalInput")
    cosK = nc.dram_tensor("cosK", [128, S], F32, kind="ExternalInput")
    sinK = nc.dram_tensor("sinK", [128, S], F32, kind="ExternalInput")
    cosQ = nc.dram_tensor("cosQ", [128, HALF], F32, kind="ExternalInput")
    sinQ = nc.dram_tensor("sinQ", [128, HALF], F32, kind="ExternalInput")
    ident16 = nc.dram_tensor("ident16", [128, 128], BF16, kind="ExternalInput")
    ident32 = nc.dram_tensor("ident32", [128, 128], F32, kind="ExternalInput")
    ones_in = nc.dram_tensor("ones_in", [128, 1], BF16, kind="ExternalInput")
    nb_in = nc.dram_tensor("nb_in", [128, 2], F32, kind="ExternalInput")
    onesk_in = nc.dram_tensor("onesk_in", [64, 128], F32, kind="ExternalInput")
    z64_in = nc.dram_tensor("z64_in", [64, 8], F32, kind="ExternalInput")
    y = nc.dram_tensor("y", [HALF, D], F32, kind="ExternalOutput")

    kw = dict(kind="ExternalOutput") if DEBUG_OUTS else {}
    xoT16_d = nc.dram_tensor("xoT16_d", [D, HALF], BF16, **kw)
    ao_d = nc.dram_tensor("ao_d", [D, HALF], BF16, **kw)
    x2_d = nc.dram_tensor("x2_d", [HALF, D], F32, **kw)
    x2q_d = nc.dram_tensor("x2q_d", [HALF, D], BF16, **kw)
    dq1_d = nc.dram_tensor("dq1_d", [1, HALF], F32, **kw)
    wgq_d = nc.dram_tensor("wgq_d", [NIC * 128, NFC * 128], BF16, **kw)
    wuq_d = nc.dram_tensor("wuq_d", [NIC * 128, NFC * 128], BF16, **kw)
    wdq_d = nc.dram_tensor("wdq_d", [NFC * 128, NIC * 128], BF16, **kw)
    dn_d = nc.dram_tensor("dn_d", [HALF, D], F32, **kw)
    stage_d = nc.dram_tensor("stage_d", [8, 2048], F32)
    if DEBUG_OUTS:
        hq_d = nc.dram_tensor("hq_d", [128, NIC * FSLAB], BF16, kind="ExternalOutput")
        shs_d = nc.dram_tensor("shs_d", [128, FSLAB], F32, kind="ExternalOutput")
        x2qf_d = nc.dram_tensor("x2qf_d", [128, NFC * FSLAB], BF16,
                                kind="ExternalOutput")
        g1_d = nc.dram_tensor("g1_d", [128, FSLAB], F32, kind="ExternalOutput")
        u1_d = nc.dram_tensor("u1_d", [128, FSLAB], F32, kind="ExternalOutput")
        si_d = nc.dram_tensor("si_d", [128, FSLAB], BF16, kind="ExternalOutput")
        hmx_d = nc.dram_tensor("hmx_d", [128, FSLAB], BF16, kind="ExternalOutput")
        hraw_d = nc.dram_tensor("hraw_d", [128, NIC * FSLAB], BF16,
                                kind="ExternalOutput")
        amax8_d = nc.dram_tensor("amax8_d", [128, 4], F32, kind="ExternalOutput")
        dqh8_d = nc.dram_tensor("dqh8_d", [128, 4], F32, kind="ExternalOutput")
        hmxc_d = nc.dram_tensor("hmxc_d", [128, FSLAB], BF16, kind="ExternalOutput")

    es = ExitStack()

    def sb(name, shape, dt):
        return es.enter_context(nc.sbuf_tensor(name, shape, dt))

    id16_sb = sb("id16_sb", [128, 128], BF16)
    id32_sb = sb("id32_sb", [128, 128], F32)
    ones_bf = sb("ones_bf", [128, 1], BF16)
    swg = sb("swg", [1, 8], F32)
    swg_bc = sb("swg_bc", [128, 8], F32)
    t11 = sb("t11", [1, 24], F32)
    nb_p = sb("nb_p", [128, 2], F32)
    onesk_sb = sb("onesk_sb", [64, 128], F32)
    z64_sb = sb("z64_sb", [64, 8], F32)

    sems = {n: es.enter_context(nc.semaphore(n)) for n in
            ("LD", "LG", "TE", "VE", "SC", "GP",
             "WA", "WB", "WC", "WD", "XA", "XB", "DA", "DB",
             "PA", "PB", "QA", "QB", "CW", "CX")}
    E = Em(nc, sems)

    pes = ExitStack()
    wstage2 = [pes.enter_context(nc.sbuf_tensor(f"wstage{i}", [128, IFF], F32))
               for i in range(2)]
    acc_p = pes.enter_context(nc.sbuf_tensor("acc_p", [128, 2], F32))
    red_p = pes.enter_context(nc.sbuf_tensor("red_p", [128, 2], F32))
    rl_p = pes.enter_context(nc.sbuf_tensor("rl_p", [1, 256], F32))
    seg_p = pes.enter_context(nc.sbuf_tensor("seg_p", [128, 128], F32))
    segb_p = pes.enter_context(nc.sbuf_tensor("segb_p", [128, 128], F32))
    segf_p = pes.enter_context(nc.sbuf_tensor("segf_p", [128, 128], F32))
    segi_p = pes.enter_context(nc.sbuf_tensor("segi_p", [128, 128], I32))
    fsg_p = pes.enter_context(nc.sbuf_tensor("fsg_p", [128, 128], F32))
    fsg2_p = pes.enter_context(nc.sbuf_tensor("fsg2_p", [128, 128], F32))
    rlf_p = pes.enter_context(nc.sbuf_tensor("rlf_p", [1, 128], F32))
    sA_2 = [pes.enter_context(nc.sbuf_tensor(f"sA_p{i}", [128, IFF], BF16))
            for i in range(2)]
    w2_2 = [pes.enter_context(nc.sbuf_tensor(f"w2_p{i}", [128, IFF], BF16))
            for i in range(2)]

    E.dma("sync", id16_sb[:], ident16[:], "LD")
    E.dma("sync", id32_sb[:], ident32[:], "LD")
    E.dma("sync", ones_bf[:], ones_in[:], "LD")
    E.dma("sync", nb_p[:], nb_in[:], "LD")
    E.dma("sync", onesk_sb[:], onesk_in[:], "LD")
    E.dma("sync", z64_sb[:], z64_in[:], "LD")
    ld0 = E.cnt["LD"]

    mats = [(wgT, wgq_d, NFC, IFF, 0, float(D * IFF)),
            (wuT, wuq_d, NFC, IFF, 1, float(D * IFF)),
            (wdT, wdq_d, NIC, D, 2, float(D * IFF))]

    def _vdrain(inc=None):
        if inc is not None:
            return E.op("vector", lambda v: v.drain(), inc=inc)
        E.op("vector", lambda v: v.drain())

    def _floor_block(dst_ap, src_ap, int_ap, r_ap, s_ap):
        ve = _vdrain(inc="VE")
        E.wait("gpsimd", "VE", ve)
        E.op("gpsimd", lambda gp: gp.tensor_copy(int_ap, src_ap))
        E.op("gpsimd", lambda gp: gp.drain())
        E.op("gpsimd", lambda gp: gp.tensor_copy(dst_ap, int_ap))
        gpv = E.op("gpsimd", lambda gp: gp.drain(), inc="GP")
        E.wait("vector", "GP", gpv)
        E.op("vector", lambda v: v.tensor_tensor(r_ap, src_ap, dst_ap,
                                                 OP.subtract))
        ve = _vdrain(inc="VE")
        E.wait("scalar", "VE", ve)
        E.op("scalar", lambda s: s.activation(s_ap, r_ap, AF.Sign))
        scv = E.op("scalar", lambda s: s.drain(), inc="SC")
        E.wait("vector", "SC", scv)
        E.op("vector", lambda v: v.tensor_scalar(s_ap, s_ap, 0.0, None, OP.min))
        _vdrain()
        E.op("vector", lambda v: v.tensor_tensor(dst_ap, dst_ap, s_ap, OP.add))
        _vdrain()

    def _t(i):
        return t11[0:1, i:i + 1]

    def _ts(di, si, imm0, imm1, op0, op1=None):
        if op1 is None:
            E.op("vector", lambda v: v.tensor_scalar(_t(di), _t(si), imm0,
                                                     imm1, op0))
        else:
            E.op("vector", lambda v: v.tensor_scalar(_t(di), _t(si), imm0,
                                                     imm1, op0, op1))
        _vdrain()

    def _tt(di, ai, bi, op):
        E.op("vector", lambda v: v.tensor_tensor(_t(di), _t(ai), _t(bi), op))
        _vdrain()

    N_IMM = float(D * IFF)
    R_IMM = float(np.float32(1.0 / (D * IFF)))

    for (src, dst, nch, wdt, mi, count) in mats:
        nseg = (wdt + 127) // 128
        ve_chunk = []
        for ch in range(nch):
            p0 = ch * 128
            pn = min(128, src.shape[0] - p0)
            ws = wstage2[ch % 2]
            psem = "PA" if ch % 2 == 0 else "PB"
            if ch >= 2:
                E.wait("sync", "VE", ve_chunk[ch - 2])
            else:
                E.wait("sync", "SC", E.cnt["SC"])
                E.wait("sync", "VE", E.cnt["VE"])
            ld = E.dma("sync", ws[0:pn, 0:wdt], src[p0:p0 + pn, :], psem)
            E.wait("vector", psem, ld)
            for sg in range(nseg):
                c0 = sg * 128
                cw = min(128, wdt - c0)
                E.op("vector", lambda v, ws=ws, pn=pn, sg=sg, c0=c0, cw=cw: v.reduce_sum(
                    seg_p[0:pn, sg:sg + 1], ws[0:pn, c0:c0 + cw], axis=AX.X,
                    apply_absolute_value=True))
            _vdrain()
            E.op("vector", lambda v, pn=pn, ns=nseg: v.tensor_scalar(
                segb_p[0:pn, 0:ns], seg_p[0:pn, 0:ns], 32.0, None, OP.mult))
            _floor_block(segf_p[0:pn, 0:nseg], segb_p[0:pn, 0:nseg],
                         segi_p[0:pn, 0:nseg], fsg_p[0:pn, 0:nseg],
                         fsg2_p[0:pn, 0:nseg])
            E.op("vector", lambda v, pn=pn, ns=nseg: v.tensor_tensor(
                segb_p[0:pn, 0:ns], segb_p[0:pn, 0:ns], segf_p[0:pn, 0:ns],
                OP.subtract))
            _vdrain()
            E.op("vector", lambda v, pn=pn, ns=nseg: v.reduce_sum(
                red_p[0:pn, 0:1], segf_p[0:pn, 0:ns], axis=AX.X))
            E.op("vector", lambda v, pn=pn, ns=nseg: v.reduce_sum(
                red_p[0:pn, 1:2], segb_p[0:pn, 0:ns], axis=AX.X))
            _vdrain()
            if ch == 0:
                E.op("vector", lambda v: v.tensor_copy(acc_p[:], red_p[:]))
            else:
                E.op("vector", lambda v, pn=pn: v.tensor_add(
                    acc_p[0:pn, 0:2], acc_p[0:pn, 0:2], red_p[0:pn, 0:2]))
            ve_chunk.append(_vdrain(inc="VE"))
        E.wait("gpsimd", "VE", ve_chunk[-1])
        E.dma("gpsimd", bass.AP(stage_d, 0, [[1, 128], [1, 1]]), acc_p[:, 0:1],
              "LG", allow_slow_non_contiguous=True)
        E.dma("gpsimd", bass.AP(stage_d, 128, [[1, 128], [1, 1]]), acc_p[:, 1:2],
              "LG", allow_slow_non_contiguous=True)
        lg = E.dma("gpsimd", rl_p[0:1, 0:256],
                   bass.AP(stage_d, 0, [[0, 1], [1, 256]]), "LG")
        E.wait("vector", "LG", lg)
        E.op("vector", lambda v: v.reduce_sum(t11[0:1, 0:1], rl_p[0:1, 0:128],
                                              axis=AX.X))
        _vdrain()
        _floor_block(rlf_p[0:1, 0:128], rl_p[0:1, 128:256],
                     segi_p[0:1, 0:128], fsg_p[0:1, 0:128], fsg2_p[0:1, 0:128])
        E.op("vector", lambda v: v.tensor_tensor(
            rl_p[0:1, 128:256], rl_p[0:1, 128:256], rlf_p[0:1, 0:128],
            OP.subtract))
        _vdrain()
        E.op("vector", lambda v: v.reduce_sum(t11[0:1, 1:2], rlf_p[0:1, 0:128],
                                              axis=AX.X))
        E.op("vector", lambda v: v.reduce_sum(t11[0:1, 2:3], rl_p[0:1, 128:256],
                                              axis=AX.X))
        _vdrain()
        _tt(3, 0, 1, OP.add)
        _tt(3, 3, 2, OP.add)
        _ts(3, 3, 1.0 / 32.0, None, OP.mult)
        _ts(4, 3, R_IMM, None, OP.mult)
        _ts(5, 4, 4097.0, None, OP.mult)
        _tt(6, 5, 4, OP.subtract)
        _tt(6, 5, 6, OP.subtract)
        _tt(7, 4, 6, OP.subtract)
        _ts(8, 4, N_IMM, None, OP.mult)
        _ts(9, 6, N_IMM, None, OP.mult)
        _tt(9, 9, 8, OP.subtract)
        _ts(10, 7, N_IMM, None, OP.mult)
        _tt(9, 9, 10, OP.add)
        _tt(10, 3, 8, OP.subtract)
        _tt(10, 10, 9, OP.subtract)
        _ts(10, 10, R_IMM, None, OP.mult)
        _tt(11, 4, 10, OP.add)
        _ts(11, 11, EPS, None, OP.max)
        E.op("vector", lambda v: v.reciprocal(_t(12), _t(11)))
        _vdrain()
        _ts(5, 11, 4097.0, None, OP.mult)
        _tt(13, 5, 11, OP.subtract)
        _tt(13, 5, 13, OP.subtract)
        _tt(14, 11, 13, OP.subtract)
        for _round in range(2):
            _ts(5, 12, 4097.0, None, OP.mult)
            _tt(15, 5, 12, OP.subtract)
            _tt(15, 5, 15, OP.subtract)
            _tt(16, 12, 15, OP.subtract)
            _tt(17, 12, 11, OP.mult)
            _tt(18, 15, 13, OP.mult)
            _tt(18, 18, 17, OP.subtract)
            _tt(19, 15, 14, OP.mult)
            _tt(18, 18, 19, OP.add)
            _tt(19, 16, 13, OP.mult)
            _tt(18, 18, 19, OP.add)
            _tt(19, 16, 14, OP.mult)
            _tt(18, 18, 19, OP.add)
            _ts(19, 17, -1.0, 1.0, OP.mult, OP.add)
            _tt(19, 19, 18, OP.subtract)
            _tt(19, 19, 12, OP.mult)
            _tt(12, 12, 19, OP.add)
        E.op("vector", lambda v, mi=mi: v.tensor_copy(
            swg[0:1, 2 * mi:2 * mi + 1], _t(12)))
        _vdrain()
        E.op("vector", lambda v, mi=mi: v.tensor_copy(
            swg[0:1, 2 * mi + 1:2 * mi + 2], _t(11)))
        ve_s = _vdrain(inc="VE")
        E.wait("gpsimd", "VE", ve_s)
        E.wait("gpsimd", "SC", E.cnt["SC"])
        E.dma("gpsimd", stage_d[0:1, 2 * mi:2 * mi + 2],
              swg[0:1, 2 * mi:2 * mi + 2], "LG")
        lg = E.dma("gpsimd", swg_bc[:, 2 * mi:2 * mi + 2],
                   bass.AP(stage_d, 2 * mi, [[0, 128], [1, 2]]), "LG")
        sc_chunk = []
        ve_q = []
        lg_q = []
        ve_xs = []

        def _qp_front(ch):
            p0 = ch * 128
            pn = min(128, src.shape[0] - p0)
            ws = wstage2[ch % 2]
            psem = "PA" if ch % 2 == 0 else "PB"
            if ch >= 2:
                E.wait("sync", "SC", sc_chunk[ch - 2])
            else:
                E.wait("sync", "SC", E.cnt["SC"])
                E.wait("sync", "VE", E.cnt["VE"])
            ld = E.dma("sync", ws[0:pn, 0:wdt], src[p0:p0 + pn, :], psem)
            E.wait("vector", psem, ld)
            E.wait("vector", "LG", lg)
            E.op("vector", lambda v, ws=ws, pn=pn, w=wdt, mi=mi: v.tensor_scalar(
                ws[0:pn, 0:w], ws[0:pn, 0:w],
                swg_bc[0:pn, 2 * mi:2 * mi + 1], None, OP.mult))
            ve_xs.append(E.op("vector", lambda v: v.drain(), inc="VE"))

        def _qp_back(ch):
            p0 = ch * 128
            pn = min(128, src.shape[0] - p0)
            ws = wstage2[ch % 2]
            sA_p = sA_2[ch % 2]
            w2_p = w2_2[ch % 2]
            cwsem = "CW" if ch % 2 == 0 else "CX"
            E.wait("scalar", "VE", ve_xs[ch])
            E.wait("scalar", "LG", lg)
            if ch >= 2:
                E.wait("scalar", "VE", ve_q[ch - 2])
                E.wait("scalar", cwsem, lg_q[ch - 2])
            else:
                E.wait("scalar", "VE", E.cnt["VE"])
                E.wait("scalar", "CW", E.cnt["CW"])
                E.wait("scalar", "CX", E.cnt["CX"])
            E.op("scalar", lambda s, ws=ws, sA_p=sA_p, pn=pn, w=wdt: s.activation(
                sA_p[0:pn, 0:w], ws[0:pn, 0:w], AF.Sign,
                bias=nb_p[0:pn, 0:1]))
            E.op("scalar", lambda s, ws=ws, w2_p=w2_p, pn=pn, w=wdt: s.activation(
                w2_p[0:pn, 0:w], ws[0:pn, 0:w], AF.Sign,
                bias=nb_p[0:pn, 1:2]))
            sc_chunk.append(E.op("scalar", lambda s: s.drain(), inc="SC"))
            E.wait("vector", "SC", sc_chunk[-1])
            E.op("vector", lambda v, sA_p=sA_p, pn=pn, w=wdt: v.tensor_scalar(
                sA_p[0:pn, 0:w], sA_p[0:pn, 0:w], 0.0, None, OP.max))
            E.op("vector", lambda v, w2_p=w2_p, pn=pn, w=wdt: v.tensor_scalar(
                w2_p[0:pn, 0:w], w2_p[0:pn, 0:w], 0.0, None, OP.min))
            ve_q.append(E.op("vector",
                             lambda v, sA_p=sA_p, w2_p=w2_p, pn=pn, w=wdt:
                             v.tensor_add(w2_p[0:pn, 0:w], sA_p[0:pn, 0:w],
                                          w2_p[0:pn, 0:w]), inc="VE"))
            E.wait("gpsimd", "VE", ve_q[-1])
            if mi < 2:
                E.dma("gpsimd",
                      bass.AP(dst, ch * 128,
                              [[NFC * 128, pn], [128 * NFC * 128, 50], [1, 128]]),
                      bass.AP(w2_p, 0, [[IFF, pn], [128, 50], [1, 128]]),
                      cwsem)
                lg2 = E.dma("gpsimd",
                            bass.AP(dst, 50 * 128 * NFC * 128 + ch * 128,
                                    [[NFC * 128, pn], [1, 64]]),
                            bass.AP(w2_p, 50 * 128, [[IFF, pn], [1, 64]]),
                            cwsem)
                lg_q.append(lg2)
            else:
                lg1 = E.dma("gpsimd",
                            bass.AP(dst, ch * 128,
                                    [[NIC * 128, pn],
                                     [128 * NIC * 128, NFC], [1, 128]]),
                            w2_p[0:pn, 0:D].rearrange("p (c j) -> p c j", j=128),
                            cwsem)
                lg_q.append(lg1)

        for ch in range(nch):
            _qp_front(ch)
            if ch >= 1:
                _qp_back(ch - 1)
        _qp_back(nch - 1)

    E.replay()
    pes.close()
    if BUILD_PHASES < 2:
        es.close()
        return nc

    aes = ExitStack()
    hes = ExitStack()
    K_sb = [hes.enter_context(nc.sbuf_tensor(f"K_sb{g}", [128, S], BF16))
            for g in range(NKV)]
    V_sb = [hes.enter_context(nc.sbuf_tensor(f"V_sb{g}", [128, S], BF16))
            for g in range(NKV)]
    cq_sb = hes.enter_context(nc.sbuf_tensor("cq_sb", [128, HALF], F32))
    sq_sb = hes.enter_context(nc.sbuf_tensor("sq_sb", [128, HALF], F32))
    mxk2 = hes.enter_context(nc.sbuf_tensor("mxk2", [1, NKV], F32))

    wkv32 = aes.enter_context(nc.sbuf_tensor("wkv32", [128, NFC * 512], F32))
    wk16 = aes.enter_context(nc.sbuf_tensor("wk16", [128, NFC * 512], BF16))
    wv16 = aes.enter_context(nc.sbuf_tensor("wv16", [128, NFC * 512], BF16))
    xstage = aes.enter_context(nc.sbuf_tensor("xstage", [128, NFC * ASLAB], F32))
    xc16 = [aes.enter_context(nc.sbuf_tensor(f"xc16_{i}", [128, NFC * ASLAB], BF16))
            for i in range(2)]
    ck_sb = [aes.enter_context(nc.sbuf_tensor(f"ck_sb{i}", [128, ASLAB], F32))
             for i in range(2)]
    sk_sb = [aes.enter_context(nc.sbuf_tensor(f"sk_sb{i}", [128, ASLAB], F32))
             for i in range(2)]
    kev_sb = aes.enter_context(nc.sbuf_tensor("kev_sb", [128, NKV * ASLAB], BF16))
    ksh_sb = aes.enter_context(nc.sbuf_tensor("ksh_sb", [128, NKV * ASLAB], BF16))
    rt_sb = aes.enter_context(nc.sbuf_tensor("rt_sb", [128, 2 * ASLAB], F32))
    ksq16 = aes.enter_context(nc.sbuf_tensor("ksq16", [128, S], BF16))
    psK = [aes.enter_context(nc.psum_tensor(f"psK{g}", [128, 512], F32))
           for g in range(NKV)]
    psV = [aes.enter_context(nc.psum_tensor(f"psV{i}", [128, 512], F32))
           for i in range(4)]

    E.dma("sync", cq_sb[:], cosQ[:], "LD")
    E.dma("sync", sq_sb[:], sinQ[:], "LD")
    ld = E.dma("sync", wkv32[:],
               bass.AP(wk, 0, [[NKV * HD, 128], [128 * NKV * HD, NFC], [1, 512]]),
               "LD")
    E.wait("vector", "LD", ld)
    vek = E.op("vector", lambda v: v.tensor_copy(wk16[:], wkv32[:]), inc="VE")
    E.wait("sync", "VE", vek)
    ld = E.dma("sync", wkv32[:],
               bass.AP(wv, 0, [[NKV * HD, 128], [128 * NKV * HD, NFC], [1, 512]]),
               "LD")
    E.wait("vector", "LD", ld)
    vev = E.op("vector", lambda v: v.tensor_copy(wv16[:], wkv32[:]), inc="VE")

    ve_cast = [None, None]
    te_cons = [None, None]
    ve_rope_done = []
    ve_vevac_done = []
    ld_tab = [None, None]
    for sl in range(NASLAB):
        buf = sl % 2
        s0 = sl * ASLAB
        if sl >= 2:
            E.wait("sync", "VE", ve_rope_done[sl - 2])
        E.dma("sync", ck_sb[buf][:, 0:ASLAB], cosK[:, s0:s0 + ASLAB], "LD")
        ld_tab[buf] = E.dma("sync", sk_sb[buf][:, 0:ASLAB], sinK[:, s0:s0 + ASLAB],
                            "LD")
        if sl >= 1:
            E.wait("sync", "VE", ve_cast[(sl - 1) % 2])
        ld = E.dma("sync", xstage[:],
                   bass.AP(xT, s0, [[S, 128], [128 * S, NFC], [1, ASLAB]]), "LD")
        E.wait("vector", "LD", ld)
        if te_cons[buf] is not None:
            E.wait("vector", "TE", te_cons[buf])
        ve_cast[buf] = E.op("vector", lambda v, b=buf: v.tensor_copy(
            xc16[b][:], xstage[:]), inc="VE")
        E.wait("tensor", "VE", ve_cast[buf])
        E.wait("tensor", "VE", vev)
        if sl >= 1:
            E.wait("tensor", "VE", ve_rope_done[sl - 1])
            E.wait("tensor", "VE", ve_vevac_done[sl - 1])
        te_k = []
        for g in range(NKV):
            for fc in range(NFC):
                def mk(t, g=g, fc=fc, b=buf):
                    return t.matmul(psK[g][:, 0:ASLAB],
                                    wk16[:, fc * 512 + g * 128: fc * 512 + (g + 1) * 128],
                                    xc16[b][:, fc * ASLAB:(fc + 1) * ASLAB],
                                    start=(fc == 0), stop=(fc == NFC - 1))
                if fc == NFC - 1:
                    te_k.append(E.op("tensor", mk, inc="TE"))
                else:
                    E.op("tensor", mk)
        te_v = []
        for i in range(2):
            for fc in range(NFC):
                def mv(t, i=i, fc=fc, b=buf):
                    return t.matmul(psV[i][:, 0:512],
                                    xc16[b][:, fc * ASLAB + i * 128: fc * ASLAB + (i + 1) * 128],
                                    wv16[:, fc * 512:(fc + 1) * 512],
                                    start=(fc == 0), stop=(fc == NFC - 1))
                if fc == NFC - 1:
                    te_v.append(E.op("tensor", mv, inc="TE"))
                else:
                    E.op("tensor", mv)
        te_cons[buf] = E.cnt["TE"]
        E.wait("vector", "LD", ld_tab[buf])
        for g in range(NKV):
            E.wait("vector", "TE", te_k[g])
            E.op("vector", lambda v, g=g: v.tensor_copy(
                kev_sb[:, g * ASLAB:(g + 1) * ASLAB], psK[g][:, 0:ASLAB]))
        ve_kev = E.op("vector", lambda v: v.drain(), inc="VE")
        E.wait("gpsimd", "VE", ve_kev)
        E.dma("gpsimd", ksh_sb[0:64, :], kev_sb[64:128, :], "LG")
        lg_ksh = E.dma("gpsimd", ksh_sb[64:128, :], kev_sb[0:64, :], "LG")
        E.wait("vector", "LG", lg_ksh)
        for g in range(NKV):
            c = ck_sb[buf]
            s_ = sk_sb[buf]
            gb = slice(g * ASLAB, (g + 1) * ASLAB)

            def r1(v, gb=gb, c=c):
                return v.tensor_mul(rt_sb[0:64, 0:ASLAB], kev_sb[0:64, gb],
                                    c[0:64, 0:ASLAB])
            E.op("vector", r1)

            def r2(v, gb=gb, s_=s_):
                return v.tensor_mul(rt_sb[0:64, ASLAB:2 * ASLAB],
                                    ksh_sb[0:64, gb], s_[0:64, 0:ASLAB])
            E.op("vector", r2)

            def r3(v, g=g, s0=s0):
                return v.tensor_sub(K_sb[g][0:64, s0:s0 + ASLAB],
                                    rt_sb[0:64, 0:ASLAB],
                                    rt_sb[0:64, ASLAB:2 * ASLAB])
            E.op("vector", r3)

            def r4(v, gb=gb, s_=s_):
                return v.tensor_mul(rt_sb[64:128, 0:ASLAB], ksh_sb[64:128, gb],
                                    s_[64:128, 0:ASLAB])
            E.op("vector", r4)

            def r5(v, gb=gb, c=c):
                return v.tensor_mul(rt_sb[64:128, ASLAB:2 * ASLAB],
                                    kev_sb[64:128, gb], c[64:128, 0:ASLAB])
            E.op("vector", r5)

            def r6(v, g=g, s0=s0):
                return v.tensor_add(K_sb[g][64:128, s0:s0 + ASLAB],
                                    rt_sb[64:128, 0:ASLAB],
                                    rt_sb[64:128, ASLAB:2 * ASLAB])
            if g == NKV - 1:
                ve_rope_done.append(E.op("vector", r6, inc="VE"))
            else:
                E.op("vector", r6)
        for i in range(2):
            E.wait("vector", "TE", te_v[i])
            kt = sl * 2 + i
            for g in range(NKV):
                def ev(v, i=i, g=g, kt=kt):
                    return v.tensor_copy(V_sb[g][:, kt * 128:(kt + 1) * 128],
                                         psV[i][:, g * 128:(g + 1) * 128])
                if i == 1 and g == NKV - 1:
                    ve_vevac_done.append(E.op("vector", ev, inc="VE"))
                else:
                    E.op("vector", ev)

    ve_xo = None
    lg_xo = None
    for osl in range(8):
        s0 = osl * ASLAB
        if ve_xo is not None:
            E.wait("sync", "VE", ve_xo)
        E.wait("sync", "VE", max(ve_cast[0] or 0, ve_cast[1] or 0))
        E.wait("sync", "TE", max(te_cons[0] or 0, te_cons[1] or 0))
        ld = E.dma("sync", xstage[:],
                   bass.AP(xoT, s0, [[HALF, 128], [128 * HALF, NFC], [1, ASLAB]]),
                   "LD")
        E.wait("vector", "LD", ld)
        if lg_xo is not None:
            E.wait("vector", "LG", lg_xo)
        ve_xo = E.op("vector", lambda v: v.tensor_copy(
            xc16[0][:], xstage[:]), inc="VE")
        E.wait("gpsimd", "VE", ve_xo)
        lg_xo = E.dma("gpsimd",
                      bass.AP(xoT16_d, s0,
                              [[HALF, 128], [128 * HALF, NFC], [1, ASLAB]]),
                      xc16[0][:], "LG")

    ve_n = E.cnt["VE"]
    for g in range(NKV):
        E.wait("scalar", "VE", ve_rope_done[-1])
        E.wait("scalar", "TE", E.cnt["TE"])
        sc = E.op("scalar", lambda s, g=g: s.activation(
            ksq16[:], K_sb[g][:], AF.Square), inc="SC")
        E.wait("tensor", "SC", sc)
        first = True
        for n in range(S // 512):
            pb = psK[n % 2]
            te = E.op("tensor", lambda t, g=g, n=n, pb=pb: t.matmul(
                pb[0:1, 0:512], ones_bf[:, 0:1],
                ksq16[:, n * 512:(n + 1) * 512], start=True, stop=True), inc="TE")
            E.wait("vector", "TE", te)
            E.op("vector", lambda v, pb=pb: v.reduce_max(
                t11[0:1, 2:3], pb[0:1, 0:512], axis=AX.X))
            E.op("vector", lambda v: v.drain())
            if first:
                E.op("vector", lambda v, g=g: v.tensor_copy(
                    mxk2[0:1, g:g + 1], t11[0:1, 2:3]))
                E.op("vector", lambda v: v.drain())
                first = False
            else:
                E.op("vector", lambda v, g=g: v.tensor_max(
                    mxk2[0:1, g:g + 1], mxk2[0:1, g:g + 1], t11[0:1, 2:3]))
                E.op("vector", lambda v: v.drain())
            ve = E.op("vector", lambda v: v.drain(), inc="VE")
            E.wait("tensor", "VE", ve)

    E.replay()
    aes.close()
    if BUILD_PHASES < 3:
        hes.close()
        es.close()
        return nc

    bes = ExitStack()
    wqs32 = [bes.enter_context(nc.sbuf_tensor(f"wqs32_{i}", [128, D], F32))
             for i in range(2)]
    wq16 = [bes.enter_context(nc.sbuf_tensor(f"wq16_{i}", [128, D], BF16))
            for i in range(2)]
    xot16 = [bes.enter_context(nc.sbuf_tensor(f"xot16_{i}", [128, HALF], BF16))
             for i in range(2)]
    Q_sb = bes.enter_context(nc.sbuf_tensor("Q_sb", [128, HALF], BF16))
    qev_sb = bes.enter_context(nc.sbuf_tensor("qev_sb", [128, HALF], BF16))
    qsh_sb = bes.enter_context(nc.sbuf_tensor("qsh_sb", [128, HALF], BF16))
    rt2_sb = bes.enter_context(nc.sbuf_tensor("rt2_sb", [128, 2048], F32))
    qsq16 = bes.enter_context(nc.sbuf_tensor("qsq16", [128, HALF], BF16))
    PT_sb = [bes.enter_context(nc.sbuf_tensor(f"PT_{i}", [128, 1024], BF16))
             for i in range(2)]
    bc_sb = bes.enter_context(nc.sbuf_tensor("bc_sb", [128, 1024], F32))
    rs_sb = bes.enter_context(nc.sbuf_tensor("rs_sb", [1, 1024], F32))
    ao16_sb = bes.enter_context(nc.sbuf_tensor("ao16_sb", [128, 1024], BF16))
    negC_sb = bes.enter_context(nc.sbuf_tensor("negC_sb", [128, 1], F32))
    ps_sc = [bes.enter_context(nc.psum_tensor(f"ps_sc{i}", [128, 1024], F32))
             for i in range(2)]
    ps_pv = bes.enter_context(nc.psum_tensor("ps_pv", [128, 1024], F32))
    ps_sum = bes.enter_context(nc.psum_tensor("ps_sum", [128, 1024], F32))

    ld_wq = [None, None]
    ve_wq16 = [None, None]
    te_q_prev = [0, 0]
    te_fc_done = [0, 0]
    sc_exp_last2 = [0, 0]
    te_pv_last = 0
    ve_ao_last = 0
    ve_recip_last = 0
    lg_ao_prev = 0
    lg_negc = 0
    lg_qsh = 0

    for h in range(min(NH, H_LIMIT)):
        g = h // GQ
        hb = h % 2
        if ve_wq16[hb] is not None:
            E.wait("sync", "VE", ve_wq16[hb])
        qsem = "QA" if hb == 0 else "QB"
        ld_wq[hb] = E.dma("sync", wqs32[hb][:],
                          bass.AP(wq, h * 128, [[D, 128], [128 * D, NFC], [1, 128]]),
                          qsem)
        E.wait("vector", qsem, ld_wq[hb])
        E.wait("vector", "TE", te_q_prev[hb])
        ve_wq16[hb] = E.op("vector", lambda v, hb=hb: v.tensor_copy(
            wq16[hb][:], wqs32[hb][:]), inc="VE")

        te_q = None
        ld_xot = [None, None]
        for fc in range(NFC):
            xb = fc % 2
            E.wait("sync", "TE", te_fc_done[xb])
            xsem = "XA" if xb == 0 else "XB"
            ld_xot[xb] = E.dma("sync", xot16[xb][:],
                               xoT16_d[fc * 128:(fc + 1) * 128, :], xsem)
            E.wait("tensor", xsem, ld_xot[xb])
            E.wait("tensor", "VE", ve_wq16[hb])
            if fc == 0:
                E.wait("tensor", "SC", sc_exp_last2[0])
                E.wait("tensor", "SC", sc_exp_last2[1])
                E.wait("tensor", "VE", ve_recip_last)
            for cc in range(4):
                pb = ps_sc[cc // 2]
                off = (cc % 2) * 512

                def mq(t, fc=fc, cc=cc, pb=pb, off=off, hb=hb, xb=xb):
                    return t.matmul(pb[:, off:off + 512],
                                    wq16[hb][:, fc * 128:(fc + 1) * 128],
                                    xot16[xb][:, cc * 512:(cc + 1) * 512],
                                    start=(fc == 0), stop=(fc == NFC - 1))
                if cc == 3:
                    te_fc_done[xb] = E.op("tensor", mq, inc="TE")
                    if fc == NFC - 1:
                        te_q = te_fc_done[xb]
                else:
                    E.op("tensor", mq)
        te_q_prev[hb] = te_q
        if H_SUB < 2:
            continue
        E.wait("vector", "TE", te_q)
        E.wait("vector", "LG", lg_qsh)
        for cc in range(2):
            E.op("vector", lambda v, cc=cc: v.tensor_copy(
                qev_sb[:, cc * 1024:(cc + 1) * 1024], ps_sc[cc][:]))
        ve_qev = E.op("vector", lambda v: v.drain(), inc="VE")
        E.wait("gpsimd", "VE", ve_qev)
        E.dma("gpsimd", qsh_sb[0:64, :], qev_sb[64:128, :], "LG")
        lg_qsh = E.dma("gpsimd", qsh_sb[64:128, :], qev_sb[0:64, :], "LG")
        E.wait("vector", "LG", lg_qsh)
        for cc in range(2):
            q0 = cc * 1024
            qs = slice(q0, q0 + 1024)

            def q1(v, qs=qs):
                return v.tensor_mul(rt2_sb[0:64, 0:1024], qev_sb[0:64, qs],
                                    cq_sb[0:64, qs])
            E.op("vector", q1)

            def q2(v, qs=qs):
                return v.tensor_mul(rt2_sb[0:64, 1024:2048], qsh_sb[0:64, qs],
                                    sq_sb[0:64, qs])
            E.op("vector", q2)

            def q3(v, qs=qs):
                return v.tensor_sub(Q_sb[0:64, qs], rt2_sb[0:64, 0:1024],
                                    rt2_sb[0:64, 1024:2048])
            E.op("vector", q3)

            def q4(v, qs=qs):
                return v.tensor_mul(rt2_sb[64:128, 0:1024], qsh_sb[64:128, qs],
                                    sq_sb[64:128, qs])
            E.op("vector", q4)

            def q5(v, qs=qs):
                return v.tensor_mul(rt2_sb[64:128, 1024:2048], qev_sb[64:128, qs],
                                    cq_sb[64:128, qs])
            E.op("vector", q5)

            def q6(v, qs=qs):
                return v.tensor_add(Q_sb[64:128, qs], rt2_sb[64:128, 0:1024],
                                    rt2_sb[64:128, 1024:2048])
            E.op("vector", q6)
        ve_rope = E.op("vector", lambda v: v.drain(), inc="VE")

        if H_SUB < 3:
            continue
        E.wait("scalar", "VE", ve_rope)
        sc_sq = E.op("scalar", lambda s: s.activation(qsq16[:], Q_sb[:], AF.Square),
                     inc="SC")
        E.wait("tensor", "SC", sc_sq)
        for j in range(4):
            off = (j % 2) * 512
            te = E.op("tensor", lambda t, j=j, off=off: t.matmul(
                ps_sum[0:1, off:off + 512], ones_bf[:, 0:1],
                qsq16[:, j * 512:(j + 1) * 512], start=True, stop=True), inc="TE")
            E.wait("vector", "TE", te)
            E.op("vector", lambda v, off=off: v.reduce_max(
                t11[0:1, 3:4], ps_sum[0:1, off:off + 512], axis=AX.X))
            E.op("vector", lambda v: v.drain())
            if j == 0:
                E.op("vector", lambda v: v.tensor_copy(t11[0:1, 4:5], t11[0:1, 3:4]))
                E.op("vector", lambda v: v.drain())
            else:
                E.op("vector", lambda v: v.tensor_max(
                    t11[0:1, 4:5], t11[0:1, 4:5], t11[0:1, 3:4]))
                E.op("vector", lambda v: v.drain())
            ve = E.op("vector", lambda v: v.drain(), inc="VE")
            E.wait("tensor", "VE", ve)
        E.op("vector", lambda v, g=g: v.tensor_add(
            t11[0:1, 5:6], t11[0:1, 4:5], mxk2[0:1, g:g + 1]))
        E.op("vector", lambda v: v.drain())
        ve_c = E.op("vector", lambda v: v.tensor_scalar(
            t11[0:1, 6:7], t11[0:1, 5:6], -0.5 * SC_SCALE, None, OP.mult), inc="VE")
        E.op("vector", lambda v: v.drain())
        if H_SUB < 4:
            continue
        E.op("vector", lambda v: v.tensor_copy(z64_sb[0:1, 0:1], t11[0:1, 6:7]))
        ve_z = E.op("vector", lambda v: v.drain(), inc="VE")
        E.wait("tensor", "VE", ve_z)
        te_nc = E.op("tensor", lambda t: t.matmul(
            ps_sum[0:128, 0:1], onesk_sb[:], z64_sb[0:64, 0:1],
            start=True, stop=True), inc="TE")
        E.wait("vector", "TE", te_nc)
        E.wait("vector", "SC", max(sc_exp_last2))
        lg_negc = 0
        ve_negc = E.op("vector", lambda v: v.tensor_copy(negC_sb[:],
                                                         ps_sum[0:128, 0:1]),
                       inc="VE")

        if H_SUB < 5:
            continue
        for qc in range(2):
            q0 = qc * 1024
            te_s = [None] * 32
            sc_e = [None] * 32
            te_p = [None] * 32

            def emit_scores(t_idx):
                buf = t_idx % 2
                if t_idx >= 2:
                    E.wait("tensor", "SC", sc_e[t_idx - 2])
                elif qc == 0:
                    pass
                else:
                    E.wait("tensor", "SC", prev_exp[buf])
                for half in range(2):
                    def ms(t, t_idx=t_idx, half=half, buf=buf, g=g, q0=q0):
                        return t.matmul(
                            ps_sc[buf][:, half * 512:(half + 1) * 512],
                            K_sb[g][:, t_idx * 128:(t_idx + 1) * 128],
                            Q_sb[:, q0 + half * 512: q0 + (half + 1) * 512],
                            start=True, stop=True)
                    if half == 1:
                        te_s[t_idx] = E.op("tensor", ms, inc="TE")
                    else:
                        E.op("tensor", ms)

            prev_exp = [sc_exp_last2[0], sc_exp_last2[1]]
            emit_scores(0)
            emit_scores(1)
            for t_idx in range(32):
                buf = t_idx % 2
                E.wait("scalar", "TE", te_s[t_idx])
                E.wait("scalar", "VE", ve_negc)
                if t_idx >= 2:
                    E.wait("scalar", "TE", te_p[t_idx - 2])
                elif te_pv_last:
                    E.wait("scalar", "TE", te_pv_last)

                def me(s, buf=buf):
                    return s.activation(PT_sb[buf][:], ps_sc[buf][:], AF.Exp,
                                        bias=negC_sb[:, 0:1], scale=SC_SCALE)
                sc_e[t_idx] = E.op("scalar", me, inc="SC")
                sc_exp_last2[buf] = sc_e[t_idx]
                if t_idx + 2 < 32:
                    emit_scores(t_idx + 2)
                E.wait("tensor", "SC", sc_e[t_idx])
                if t_idx == 0:
                    E.wait("tensor", "VE", ve_ao_last)
                    E.wait("tensor", "VE", ve_recip_last)
                for half in range(2):
                    def mp(t, t_idx=t_idx, half=half, buf=buf, g=g):
                        return t.matmul(
                            ps_pv[:, half * 512:(half + 1) * 512],
                            V_sb[g][:, t_idx * 128:(t_idx + 1) * 128],
                            PT_sb[buf][:, half * 512:(half + 1) * 512],
                            start=(t_idx == 0), stop=(t_idx == 31))
                    E.op("tensor", mp)
                for half in range(2):
                    def mu(t, t_idx=t_idx, half=half, buf=buf):
                        return t.matmul(
                            ps_sum[0:1, half * 512:(half + 1) * 512],
                            ones_bf[:, 0:1],
                            PT_sb[buf][:, half * 512:(half + 1) * 512],
                            start=(t_idx == 0), stop=(t_idx == 31))
                    if half == 1:
                        te_p[t_idx] = E.op("tensor", mu, inc="TE")
                    else:
                        E.op("tensor", mu)
            te_pv_last = te_p[31]
            if H_SUB < 6:
                continue
            E.wait("vector", "TE", te_pv_last)
            ve_recip_last = E.op("vector", lambda v: v.reciprocal(
                rs_sb[:], ps_sum[0:1, :]), inc="VE")
            E.wait("gpsimd", "VE", ve_recip_last)
            E.dma("gpsimd", stage_d[2:3, 0:1024], rs_sb[:], "LG")
            lg_bc = E.dma("gpsimd", bc_sb[:],
                          bass.AP(stage_d, 2 * 2048, [[0, 128], [1, 1024]]), "LG")
            E.wait("vector", "LG", lg_bc)
            E.wait("vector", "LG", lg_ao_prev)
            ve_ao_last = E.op("vector", lambda v: v.tensor_mul(
                ao16_sb[:], ps_pv[:], bc_sb[:]), inc="VE")
            E.wait("gpsimd", "VE", ve_ao_last)
            lg_ao_prev = E.dma("gpsimd",
                               ao_d[h * 128:(h + 1) * 128, q0:q0 + 1024],
                               ao16_sb[:], "LG")

    E.replay()
    bes.close()
    hes.close()
    if BUILD_PHASES < 4:
        es.close()
        return nc

    wes = ExitStack()
    wo16 = wes.enter_context(nc.sbuf_tensor("wo16", [128, NFC * D], BF16))
    wof32 = wes.enter_context(nc.sbuf_tensor("wof32", [128, D], F32))
    aop = [wes.enter_context(nc.sbuf_tensor(f"aop{i}", [128, NFC * 128], BF16))
           for i in range(2)]
    xt_sb = [wes.enter_context(nc.sbuf_tensor(f"xt_sb{i}", [128, D], F32))
             for i in range(2)]
    res1_sb = wes.enter_context(nc.sbuf_tensor("res1_sb", [128, D], F32))
    x2_sb = wes.enter_context(nc.sbuf_tensor("x2_sb", [128, D], F32))
    v_sb = wes.enter_context(nc.sbuf_tensor("v_sb", [128, D], F32))
    vi_sb = wes.enter_context(nc.sbuf_tensor("vi_sb", [128, D], I32))
    vr_sb = wes.enter_context(nc.sbuf_tensor("vr_sb", [128, D], F32))
    x2q_sb = wes.enter_context(nc.sbuf_tensor("x2q_sb", [128, D], BF16))
    d_sb = wes.enter_context(nc.sbuf_tensor("d_sb", [128, D], F32))
    dsq16 = wes.enter_context(nc.sbuf_tensor("dsq16", [128, D], BF16))
    lnt = wes.enter_context(nc.sbuf_tensor("lnt", [128, 8], F32))
    ps_wo = [wes.enter_context(nc.psum_tensor(f"ps_wo{i}", [128, 2048], F32))
             for i in range(2)]

    ve_woc = None
    for fc in range(NFC):
        if ve_woc is not None:
            E.wait("sync", "VE", ve_woc)
        ld = E.dma("sync", wof32[:], wo[fc * 128:(fc + 1) * 128, :], "LD")
        E.wait("vector", "LD", ld)
        ve_woc = E.op("vector", lambda v, fc=fc: v.tensor_copy(
            wo16[:, fc * D:(fc + 1) * D], wof32[:]), inc="VE")

    ve_tt_done = [0, 0]
    te_wo_done = [0, 0]
    lg_tt_done = [0, 0]
    for tt in range(NFC):
        tb = tt % 2
        r0 = tt * 128
        E.wait("sync", "VE", ve_tt_done[tb])
        E.wait("sync", "TE", te_wo_done[tb])
        wsem = "WA" if tb == 0 else "WB"
        ld_ao = E.dma("sync", aop[tb][:],
                      bass.AP(ao_d, r0, [[HALF, 128], [128 * HALF, NFC], [1, 128]]),
                      wsem)
        ld_xt = E.dma("sync", xt_sb[tb][:], x_tok[r0:r0 + 128, :], wsem)
        E.wait("tensor", wsem, ld_ao)
        E.wait("tensor", "VE", ve_woc)
        if tt >= 2:
            E.wait("tensor", "VE", ve_tt_done[tb])
        te_wo = None
        for c in range(NFC):
            for n in range(4):
                def mw(t, c=c, n=n, tb=tb):
                    return t.matmul(ps_wo[tb][:, n * 512:(n + 1) * 512],
                                    aop[tb][:, c * 128:(c + 1) * 128],
                                    wo16[:, c * D + n * 512: c * D + (n + 1) * 512],
                                    start=(c == 0), stop=(c == NFC - 1))
                if c == NFC - 1 and n == 3:
                    te_wo = E.op("tensor", mw, inc="TE")
                else:
                    E.op("tensor", mw)
        te_wo_done[tb] = te_wo
        E.wait("vector", "TE", te_wo)
        E.wait("vector", wsem, ld_xt)
        E.wait("vector", "LG", lg_tt_done[1 - tb])
        E.op("vector", lambda v, tb=tb: v.scalar_tensor_tensor(
            res1_sb[:], ps_wo[tb][:], 1.0, xt_sb[tb][:], OP.mult, OP.add))
        E.op("vector", lambda v: v.reduce_sum(lnt[:, 0:1], res1_sb[:], axis=AX.X))
        E.op("vector", lambda v: v.drain())
        E.op("vector", lambda v: v.tensor_scalar(lnt[:, 0:1], lnt[:, 0:1],
                                                 1.0 / D, None, OP.mult))
        E.op("vector", lambda v: v.drain())
        ve_d = E.op("vector", lambda v: v.tensor_scalar(
            d_sb[:], res1_sb[:], lnt[:, 0:1], None, OP.subtract), inc="VE")
        E.wait("scalar", "VE", ve_d)
        sc_sq = E.op("scalar", lambda s: s.activation(dsq16[:], d_sb[:],
                                                      AF.Square), inc="SC")
        E.wait("vector", "SC", sc_sq)
        E.op("vector", lambda v: v.reduce_sum(lnt[:, 3:4], dsq16[:], axis=AX.X))
        E.op("vector", lambda v: v.drain())
        E.op("vector", lambda v: v.tensor_scalar(
            lnt[:, 3:4], lnt[:, 3:4], 1.0 / D, EPS, OP.mult, OP.add))
        E.op("vector", lambda v: v.drain())
        ve_r = E.op("vector", lambda v: v.reciprocal(lnt[:, 4:5], lnt[:, 3:4]),
                    inc="VE")
        E.wait("scalar", "VE", ve_r)
        sc_rs = E.op("scalar", lambda s: s.activation(lnt[:, 5:6], lnt[:, 4:5],
                                                      AF.Sqrt), inc="SC")
        E.wait("vector", "SC", sc_rs)
        E.op("vector", lambda v: v.tensor_scalar(
            x2_sb[:], d_sb[:], lnt[:, 5:6], None, OP.mult))
        E.op("vector", lambda v: v.reduce_max(lnt[:, 6:7], x2_sb[:], axis=AX.X,
                                              apply_absolute_value=True))
        E.op("vector", lambda v: v.drain())
        E.op("vector", lambda v: v.tensor_scalar(lnt[:, 6:7], lnt[:, 6:7], EPS,
                                                 None, OP.max))
        E.op("vector", lambda v: v.drain())
        E.op("vector", lambda v: v.reciprocal(lnt[:, 7:8], lnt[:, 6:7]))
        E.op("vector", lambda v: v.drain())
        E.op("vector", lambda v: v.tensor_scalar(lnt[:, 7:8], lnt[:, 7:8], 127.0,
                                                 None, OP.mult))
        E.op("vector", lambda v: v.drain())
        E.op("vector", lambda v: v.tensor_scalar(lnt[:, 6:7], lnt[:, 6:7],
                                                 1.0 / 127.0, None, OP.mult))
        E.op("vector", lambda v: v.drain())
        ve_v = E.op("vector", lambda v: v.tensor_scalar(
            v_sb[:], x2_sb[:], lnt[:, 7:8], RND, OP.mult, OP.add), inc="VE")
        E.wait("gpsimd", "VE", ve_v)
        E.op("gpsimd", lambda gp: gp.tensor_copy(vi_sb[:], v_sb[:]))
        E.op("gpsimd", lambda gp: gp.drain())
        E.op("gpsimd", lambda gp: gp.tensor_copy(vr_sb[:], vi_sb[:]))
        gp_r = E.op("gpsimd", lambda gp: gp.drain(), inc="GP")
        E.wait("vector", "GP", gp_r)
        E.op("vector", lambda v: v.tensor_tensor(v_sb[:], v_sb[:], vr_sb[:],
                                                 OP.subtract))
        ve_rr = E.op("vector", lambda v: v.drain(), inc="VE")
        E.wait("scalar", "VE", ve_rr)
        E.op("scalar", lambda s: s.activation(dsq16[:], v_sb[:], AF.Sign))
        sc_sg = E.op("scalar", lambda s: s.drain(), inc="SC")
        E.wait("vector", "SC", sc_sg)
        E.op("vector", lambda v: v.tensor_scalar(dsq16[:], dsq16[:], 0.0, None,
                                                 OP.min))
        E.op("vector", lambda v: v.tensor_tensor(v_sb[:], vr_sb[:], dsq16[:],
                                                 OP.add))
        ve_q = E.op("vector", lambda v: v.tensor_scalar(
            x2q_sb[:], v_sb[:], -384.0, 127.0, OP.add, OP.min), inc="VE")
        E.wait("gpsimd", "VE", ve_q)
        E.dma("gpsimd", x2_d[r0:r0 + 128, :], x2_sb[:], "LG")
        E.dma("gpsimd", x2q_d[r0:r0 + 128, :], x2q_sb[:], "LG")
        lg_tt_done[tb] = E.dma("gpsimd", dq1_d[0:1, r0:r0 + 128], lnt[:, 6:7],
                               "LG", allow_slow_non_contiguous=True)
        ve_tt_done[tb] = ve_q

    E.replay()
    wes.close()
    if BUILD_PHASES < 5:
        es.close()
        return nc

    fes = ExitStack()
    h_sb = fes.enter_context(nc.sbuf_tensor("h_sb", [128, NIC * FSLAB], BF16))
    x2qf = fes.enter_context(nc.sbuf_tensor("x2qf", [128, NFC * FSLAB], BF16))
    x2qrow = [fes.enter_context(nc.sbuf_tensor(f"x2qrow{i}", [128, D], BF16))
              for i in range(2)]
    wgt = [fes.enter_context(nc.sbuf_tensor(f"wgt{i}", [128, NFC * 128], BF16))
           for i in range(2)]
    wut = [fes.enter_context(nc.sbuf_tensor(f"wut{i}", [128, NFC * 128], BF16))
           for i in range(2)]
    wdt_sb = [fes.enter_context(nc.sbuf_tensor(f"wdt{i}", [128, NIC * 128], BF16))
              for i in range(2)]
    g1_sb = fes.enter_context(nc.sbuf_tensor("g1_sb", [128, FSLAB], F32))
    u1_sb = fes.enter_context(nc.sbuf_tensor("u1_sb", [128, FSLAB], F32))
    silu_sb = fes.enter_context(nc.sbuf_tensor("silu_sb", [128, FSLAB], BF16))
    hmx_sb = fes.enter_context(nc.sbuf_tensor("hmx_sb", [128, FSLAB], BF16))
    hmn_sb = fes.enter_context(nc.sbuf_tensor("hmn_sb", [128, FSLAB], BF16))
    QG = 2
    vf_sb = fes.enter_context(nc.sbuf_tensor("vf_sb", [128, QG * FSLAB], F32))
    vfi_sb = fes.enter_context(nc.sbuf_tensor("vfi_sb", [128, QG * FSLAB], I32))
    vfr_sb = fes.enter_context(nc.sbuf_tensor("vfr_sb", [128, QG * FSLAB], F32))
    qsg_sb = fes.enter_context(nc.sbuf_tensor("qsg_sb", [128, QG * FSLAB], BF16))
    shs_rp = fes.enter_context(nc.sbuf_tensor("shs_rp", [128, QG * FSLAB], F32))
    gsc_bc = fes.enter_context(nc.sbuf_tensor("gsc_bc", [128, FSLAB], F32))
    usc_bc = fes.enter_context(nc.sbuf_tensor("usc_bc", [128, FSLAB], F32))
    shs_bc = fes.enter_context(nc.sbuf_tensor("shs_bc", [128, FSLAB], F32))
    row_sb = fes.enter_context(nc.sbuf_tensor("row_sb", [1, FSLAB], F32))
    amax8 = fes.enter_context(nc.sbuf_tensor("amax8", [128, 4], F32))
    dqh8 = fes.enter_context(nc.sbuf_tensor("dqh8", [128, 4], F32))
    dnf_sb = fes.enter_context(nc.sbuf_tensor("dnf_sb", [128, FSLAB], F32))
    dnT_sb = fes.enter_context(nc.sbuf_tensor("dnT_sb", [128, 4 * 128], F32))
    dnrow = [fes.enter_context(nc.sbuf_tensor(f"dnrow{i}", [128, D], F32))
             for i in range(2)]
    x2row = [fes.enter_context(nc.sbuf_tensor(f"x2row{i}", [128, D], F32))
             for i in range(2)]
    yout_sb = fes.enter_context(nc.sbuf_tensor("yout_sb", [128, D], F32))
    dsqf16 = fes.enter_context(nc.sbuf_tensor("dsqf16", [128, D], BF16))
    lntf = fes.enter_context(nc.sbuf_tensor("lntf", [128, 8], F32))
    dcomb = fes.enter_context(nc.sbuf_tensor("dcomb", [128, 1], F32))
    ps_g = fes.enter_context(nc.psum_tensor("ps_g", [128, 512], F32))
    ps_u = fes.enter_context(nc.psum_tensor("ps_u", [128, 512], F32))
    ps_d = fes.enter_context(nc.psum_tensor("ps_d", [128, 512], F32))
    ps_tp = [fes.enter_context(nc.psum_tensor(f"ps_tp{i}", [128, 1024], BF16))
             for i in range(2)]
    ps_tf = fes.enter_context(nc.psum_tensor("ps_tf", [128, 512], F32))

    NT8 = FSLAB // 128
    y_st = []
    te_x2q = [0, 0]
    te_gu_last = 0
    te_dn_last = 0
    ve_ln2 = [0, 0]
    for sl in range(NFSLAB):
        s0 = sl * FSLAB
        ve_tpev = E.cnt["VE"]
        for t8 in range(NT8):
            xb = t8 % 2
            E.wait("sync", "TE", te_x2q[xb])
            xsem = "XA" if xb == 0 else "XB"
            ld = E.dma("sync", x2qrow[xb][:],
                       x2q_d[s0 + t8 * 128: s0 + (t8 + 1) * 128, :], xsem)
            E.wait("tensor", xsem, ld)
            if sl > 0:
                E.wait("tensor", "TE", te_gu_last)
            for fc in range(NFC):
                pb = ps_tp[fc % 2]
                te = E.op("tensor", lambda t, fc=fc, xb=xb, pb=pb: t.transpose(
                    pb[:, 0:128], x2qrow[xb][:, fc * 128:(fc + 1) * 128],
                    id16_sb[:]), inc="TE")
                te_x2q[xb] = te
                E.wait("vector", "TE", te)
                ve_tpev = E.op("vector", lambda v, fc=fc, t8=t8, pb=pb: v.tensor_copy(
                    x2qf[:, fc * FSLAB + t8 * 128: fc * FSLAB + (t8 + 1) * 128],
                    pb[:, 0:128]), inc="VE")
                E.wait("tensor", "VE", ve_tpev)
        E.wait("sync", "VE", E.cnt["VE"])
        E.wait("sync", "LG", E.cnt["LG"])
        ld = E.dma("sync", row_sb[:], dq1_d[0:1, s0:s0 + FSLAB], "LD")
        E.wait("vector", "LD", ld)
        E.op("vector", lambda v: v.tensor_scalar(
            row_sb[:], row_sb[:], swg[0:1, 1:2], None, OP.mult))
        E.op("vector", lambda v: v.drain())
        ve = E.op("vector", lambda v: v.drain(), inc="VE")
        E.wait("gpsimd", "VE", ve)
        E.dma("gpsimd", stage_d[3:4, 0:FSLAB], row_sb[:], "LG")
        lg = E.dma("gpsimd", gsc_bc[:],
                   bass.AP(stage_d, 3 * 2048, [[0, 128], [1, FSLAB]]), "LG")
        E.wait("sync", "VE", ve)
        E.wait("sync", "LG", lg)
        ld = E.dma("sync", row_sb[:], dq1_d[0:1, s0:s0 + FSLAB], "LD")
        E.wait("vector", "LD", ld)
        E.wait("vector", "LG", lg)
        E.op("vector", lambda v: v.tensor_scalar(
            row_sb[:], row_sb[:], swg[0:1, 3:4], None, OP.mult))
        E.op("vector", lambda v: v.drain())
        ve = E.op("vector", lambda v: v.drain(), inc="VE")
        E.wait("gpsimd", "VE", ve)
        E.dma("gpsimd", stage_d[3:4, 0:FSLAB], row_sb[:], "LG")
        lg_usc = E.dma("gpsimd", usc_bc[:],
                       bass.AP(stage_d, 3 * 2048, [[0, 128], [1, FSLAB]]), "LG")

        ve_g1 = 0
        ve_u1 = 0
        ve_h = 0
        ld_w = [0, 0]
        for ic in range(NIC):
            wb = ic % 2
            pn = ICW[ic]
            if ic >= 2:
                E.wait("sync", "TE", te_gu[wb])
            wsem = "WA" if wb == 0 else "WB"
            if pn == 128:
                E.dma("sync", wgt[wb][:, 0:NFC * 128],
                      bass.AP(wgq_d, ic * 128 * NFC * 128,
                              [[NFC * 128, 128], [1, NFC * 128]]), wsem)
                ld_w[wb] = E.dma("sync", wut[wb][:, 0:NFC * 128],
                                 bass.AP(wuq_d, ic * 128 * NFC * 128,
                                         [[NFC * 128, 128], [1, NFC * 128]]),
                                 wsem)
            else:
                E.dma("sync",
                      wgt[wb][:, 0:NFC * pn].rearrange("p (c j) -> p c j", j=pn),
                      bass.AP(wgq_d, ic * 128 * NFC * 128,
                              [[NFC * 128, 128], [128, NFC], [1, pn]]), wsem)
                ld_w[wb] = E.dma("sync",
                                 wut[wb][:, 0:NFC * pn].rearrange(
                                     "p (c j) -> p c j", j=pn),
                                 bass.AP(wuq_d, ic * 128 * NFC * 128,
                                         [[NFC * 128, 128], [128, NFC], [1, pn]]),
                                 wsem)
            E.wait("tensor", wsem, ld_w[wb])
            E.wait("tensor", "VE", ve_g1)
            if ic == 0:
                E.wait("tensor", "VE", ve_tpev)
            for fc in range(NFC):
                def mg(t, ic=ic, fc=fc, wb=wb, pn=pn):
                    return t.matmul(ps_g[0:pn, 0:FSLAB],
                                    wgt[wb][:, fc * pn: (fc + 1) * pn],
                                    x2qf[:, fc * FSLAB:(fc + 1) * FSLAB],
                                    start=(fc == 0), stop=(fc == NFC - 1))
                if fc == NFC - 1:
                    te_gate = E.op("tensor", mg, inc="TE")
                else:
                    E.op("tensor", mg)
            E.wait("tensor", "VE", ve_u1)
            for fc in range(NFC):
                def mu2(t, ic=ic, fc=fc, wb=wb, pn=pn):
                    return t.matmul(ps_u[0:pn, 0:FSLAB],
                                    wut[wb][:, fc * pn: (fc + 1) * pn],
                                    x2qf[:, fc * FSLAB:(fc + 1) * FSLAB],
                                    start=(fc == 0), stop=(fc == NFC - 1))
                if fc == NFC - 1:
                    te_up = E.op("tensor", mu2, inc="TE")
                else:
                    E.op("tensor", mu2)
            if ic == 0:
                te_gu = [None, None]
            te_gu[wb] = te_up
            te_gu_last = te_up
            E.wait("vector", "TE", te_gate)
            E.wait("vector", "LG", lg_usc)
            E.op("vector", lambda v, pn=pn: v.tensor_mul(
                g1_sb[0:pn, :], ps_g[0:pn, 0:FSLAB], gsc_bc[0:pn, :]))
            ve_g1 = E.op("vector", lambda v: v.drain(), inc="VE")
            E.wait("scalar", "VE", ve_g1)
            E.op("scalar", lambda s, pn=pn: s.activation(
                silu_sb[0:pn, :], g1_sb[0:pn, :], AF.Silu))
            sc_si = E.op("scalar", lambda s: s.drain(), inc="SC")
            E.wait("vector", "TE", te_up)
            E.op("vector", lambda v, pn=pn: v.tensor_mul(
                u1_sb[0:pn, :], ps_u[0:pn, 0:FSLAB], usc_bc[0:pn, :]))
            ve_u1 = E.op("vector", lambda v: v.drain(), inc="VE")
            E.wait("vector", "SC", sc_si)
            if sl > 0 and ic == 0:
                E.wait("vector", "TE", te_dn_last)
            E.op("vector", lambda v, ic=ic, pn=pn: v.tensor_mul(
                h_sb[0:pn, ic * FSLAB:(ic + 1) * FSLAB], silu_sb[0:pn, :],
                u1_sb[0:pn, :]))
            if DEBUG_OUTS and sl == DBG_SL and ic == 12:
                ve_dbg = E.op("vector", lambda v: v.drain(), inc="VE")
                E.wait("gpsimd", "VE", ve_dbg)
                E.dma("gpsimd", g1_d[:], g1_sb[:], "LG")
                E.dma("gpsimd", u1_d[:], u1_sb[:], "LG")
                E.dma("gpsimd", si_d[:], silu_sb[:], "LG")
                E.wait("vector", "LG", E.cnt["LG"])
                E.wait("scalar", "LG", E.cnt["LG"])
            if ic == 0:
                E.op("vector", lambda v: v.tensor_copy(hmx_sb[:], h_sb[:, 0:FSLAB]))
                E.op("vector", lambda v: v.tensor_copy(
                    hmn_sb[:], h_sb[:, 0:FSLAB]))
                ve_h = E.op("vector", lambda v: v.drain(), inc="VE")
            else:
                E.op("vector", lambda v, ic=ic, pn=pn: v.tensor_tensor(
                    hmx_sb[0:pn, :], hmx_sb[0:pn, :],
                    h_sb[0:pn, ic * FSLAB:(ic + 1) * FSLAB], OP.max))
                E.op("vector", lambda v, ic=ic, pn=pn: v.tensor_tensor(
                    hmn_sb[0:pn, :], hmn_sb[0:pn, :],
                    h_sb[0:pn, ic * FSLAB:(ic + 1) * FSLAB], OP.min))
                ve_h = E.op("vector", lambda v: v.drain(), inc="VE")

        if DEBUG_OUTS and sl == DBG_SL:
            ve_dbg2 = E.op("vector", lambda v: v.drain(), inc="VE")
            E.wait("gpsimd", "VE", ve_dbg2)
            E.dma("gpsimd", hraw_d[:], h_sb[:], "LG")
            E.dma("gpsimd", hmx_d[:], hmx_sb[:], "LG")
            E.wait("vector", "LG", E.cnt["LG"])
        E.wait("vector", "VE", ve_h)
        E.op("vector", lambda v: v.tensor_scalar(
            hmn_sb[:], hmn_sb[:], -1.0, None, OP.mult))
        E.op("vector", lambda v: v.tensor_tensor(
            hmx_sb[:], hmx_sb[:], hmn_sb[:], OP.max))
        ve_h = E.op("vector", lambda v: v.drain(), inc="VE")
        if DEBUG_OUTS and sl == DBG_SL:
            E.wait("gpsimd", "VE", ve_h)
            E.dma("gpsimd", hmxc_d[:], hmx_sb[:], "LG")
            E.wait("vector", "LG", E.cnt["LG"])
            E.wait("tensor", "LG", E.cnt["LG"])
        E.wait("tensor", "VE", ve_h)
        for t8 in range(NT8):
            pb = ps_tp[t8 % 2]
            te = E.op("tensor", lambda t, t8=t8, pb=pb: t.transpose(
                pb[:, 0:128], hmx_sb[:, t8 * 128:(t8 + 1) * 128], id16_sb[:]),
                inc="TE")
            E.wait("vector", "TE", te)
            E.op("vector", lambda v, t8=t8, pb=pb: v.reduce_max(
                amax8[:, t8:t8 + 1], pb[:, 0:128], axis=AX.X))
            E.op("vector", lambda v: v.drain())
            E.op("vector", lambda v, t8=t8: v.tensor_scalar(
                amax8[:, t8:t8 + 1], amax8[:, t8:t8 + 1], EPS, None, OP.max))
            E.op("vector", lambda v: v.drain())
            E.op("vector", lambda v, t8=t8: v.tensor_mul(
                dqh8[:, t8:t8 + 1], amax8[:, t8:t8 + 1], swg_bc[:, 5:6]))
            E.op("vector", lambda v: v.drain())
            E.op("vector", lambda v, t8=t8: v.tensor_scalar(
                dqh8[:, t8:t8 + 1], dqh8[:, t8:t8 + 1], 1.0 / 127.0, None,
                OP.mult))
            E.op("vector", lambda v: v.drain())
            ve = E.op("vector", lambda v: v.drain(), inc="VE")
            E.wait("tensor", "VE", ve)
            E.wait("gpsimd", "VE", ve)
            E.dma("gpsimd", stage_d[4:5, t8 * 128:(t8 + 1) * 128],
                  amax8[:, t8:t8 + 1], "LG", allow_slow_non_contiguous=True)
        E.wait("gpsimd", "VE", E.cnt["VE"])
        if DEBUG_OUTS and sl == DBG_SL:
            E.dma("gpsimd", amax8_d[:], amax8[:], "LG")
            E.dma("gpsimd", dqh8_d[:], dqh8[:], "LG")
            E.wait("vector", "LG", E.cnt["LG"])
        lg = E.dma("gpsimd", row_sb[:], stage_d[4:5, 0:FSLAB], "LG")
        E.wait("vector", "LG", lg)
        E.op("vector", lambda v: v.reciprocal(row_sb[:], row_sb[:]))
        E.op("vector", lambda v: v.drain())
        E.op("vector", lambda v: v.tensor_scalar(row_sb[:], row_sb[:], 127.0,
                                                 None, OP.mult))
        E.op("vector", lambda v: v.drain())
        ve = E.op("vector", lambda v: v.drain(), inc="VE")
        E.wait("gpsimd", "VE", ve)
        E.dma("gpsimd", stage_d[5:6, 0:FSLAB], row_sb[:], "LG")
        lg_sh = E.dma("gpsimd", shs_bc[:],
                      bass.AP(stage_d, 5 * 2048, [[0, 128], [1, FSLAB]]), "LG")
        lg_shr = E.dma("gpsimd", shs_rp[:],
                       bass.AP(stage_d, 5 * 2048,
                               [[0, 128], [0, QG], [1, FSLAB]]), "LG")

        qgroups = []
        hq_done = [0] * NIC
        _g = 0
        while _g < NIC:
            _n = min(QG, NIC - _g)
            while ICW[_g + _n - 1] != ICW[_g]:
                _n -= 1
            qgroups.append((_g, _n))
            _g += _n
        E.wait("vector", "LG", lg_shr)
        for (g0, gn) in qgroups:
            pn = ICW[g0]
            cw = gn * FSLAB
            c0 = g0 * FSLAB
            E.op("vector", lambda v, pn=pn, cw=cw, c0=c0: v.tensor_mul(
                vf_sb[0:pn, 0:cw], h_sb[0:pn, c0:c0 + cw], shs_rp[0:pn, 0:cw]))
            E.op("vector", lambda v, pn=pn, cw=cw: v.tensor_scalar(
                vf_sb[0:pn, 0:cw], vf_sb[0:pn, 0:cw], RND, None, OP.add))
            ve = E.op("vector", lambda v: v.drain(), inc="VE")
            E.wait("gpsimd", "VE", ve)
            E.op("gpsimd", lambda gp, pn=pn, cw=cw: gp.tensor_copy(
                vfi_sb[0:pn, 0:cw], vf_sb[0:pn, 0:cw]))
            E.op("gpsimd", lambda gp: gp.drain())
            E.op("gpsimd", lambda gp, pn=pn, cw=cw: gp.tensor_copy(
                vfr_sb[0:pn, 0:cw], vfi_sb[0:pn, 0:cw]))
            gp_hq = E.op("gpsimd", lambda gp: gp.drain(), inc="GP")
            E.wait("vector", "GP", gp_hq)
            E.op("vector", lambda v, pn=pn, cw=cw: v.tensor_tensor(
                vf_sb[0:pn, 0:cw], vf_sb[0:pn, 0:cw], vfr_sb[0:pn, 0:cw],
                OP.subtract))
            ve_rr = E.op("vector", lambda v: v.drain(), inc="VE")
            E.wait("scalar", "VE", ve_rr)
            E.op("scalar", lambda s, pn=pn, cw=cw: s.activation(
                qsg_sb[0:pn, 0:cw], vf_sb[0:pn, 0:cw], AF.Sign))
            sc_sg = E.op("scalar", lambda s: s.drain(), inc="SC")
            E.wait("vector", "SC", sc_sg)
            E.op("vector", lambda v, pn=pn, cw=cw: v.tensor_scalar(
                qsg_sb[0:pn, 0:cw], qsg_sb[0:pn, 0:cw], 0.0, None, OP.min))
            E.op("vector", lambda v, pn=pn, cw=cw: v.tensor_tensor(
                vf_sb[0:pn, 0:cw], vfr_sb[0:pn, 0:cw], qsg_sb[0:pn, 0:cw],
                OP.add))
            E.op("vector", lambda v, pn=pn, cw=cw, c0=c0: v.tensor_scalar(
                h_sb[0:pn, c0:c0 + cw], vf_sb[0:pn, 0:cw],
                -384.0, 127.0, OP.add, OP.min))
            m = E.op("vector", lambda v: v.drain(), inc="VE")
            for _ic in range(g0, g0 + gn):
                hq_done[_ic] = m
        ve_hq = hq_done[NIC - 1]

        if DEBUG_OUTS and sl == DBG_SL:
            E.wait("gpsimd", "VE", E.cnt["VE"])
            E.dma("gpsimd", hq_d[:], h_sb[:], "LG")
            E.dma("gpsimd", shs_d[:], shs_bc[:], "LG")
            E.dma("gpsimd", x2qf_d[:], x2qf[:], "LG")
            E.wait("vector", "LG", E.cnt["LG"])
            E.wait("tensor", "LG", E.cnt["LG"])
        lg_dn = 0
        ve_dn = 0
        for dc in range(NFC):
            wb = dc % 2
            if dc >= 2:
                E.wait("sync", "TE", te_dn[wb])
            dsem = "WC" if wb == 0 else "WD"
            E.dma("sync", wdt_sb[wb][:, 0:50 * 128],
                  bass.AP(wdq_d, dc * 128 * NIC * 128,
                          [[NIC * 128, 128], [1, 50 * 128]]), dsem)
            ld_wd = E.dma("sync", wdt_sb[wb][0:64, 50 * 128:51 * 128],
                          bass.AP(wdq_d, dc * 128 * NIC * 128 + 50 * 128,
                                  [[NIC * 128, 64], [1, 128]]), dsem)
            E.wait("tensor", dsem, ld_wd)
            E.wait("tensor", "VE", ve_dn)
            for ic in range(NIC):
                pn = ICW[ic]
                E.wait("tensor", "VE", hq_done[ic])

                def md(t, ic=ic, dc=dc, wb=wb, pn=pn):
                    return t.matmul(ps_d[:, 0:FSLAB],
                                    wdt_sb[wb][0:pn, ic * 128:(ic + 1) * 128],
                                    h_sb[0:pn, ic * FSLAB:(ic + 1) * FSLAB],
                                    start=(ic == 0), stop=(ic == NIC - 1))
                if ic == NIC - 1:
                    te_d = E.op("tensor", md, inc="TE")
                else:
                    E.op("tensor", md)
            if dc == 0:
                te_dn = [None, None]
            te_dn[wb] = te_d
            te_dn_last = te_d
            E.wait("vector", "TE", te_d)
            E.op("vector", lambda v: v.tensor_copy(dnf_sb[:], ps_d[:, 0:FSLAB]))
            ve_dn = E.op("vector", lambda v: v.drain(), inc="VE")
            E.wait("tensor", "VE", ve_dn)
            if lg_dn:
                E.wait("tensor", "LG", lg_dn)
            for t8 in range(NT8):
                te = E.op("tensor", lambda t, t8=t8: t.transpose(
                    ps_tf[:, 0:128], dnf_sb[:, t8 * 128:(t8 + 1) * 128],
                    id32_sb[:]), inc="TE")
                E.wait("vector", "TE", te)
                E.op("vector", lambda v, t8=t8: v.tensor_copy(
                    dnT_sb[:, t8 * 128:(t8 + 1) * 128], ps_tf[:, 0:128]))
                ve = E.op("vector", lambda v: v.drain(), inc="VE")
                E.wait("tensor", "VE", ve)
            E.wait("gpsimd", "VE", E.cnt["VE"])
            lg_dn = E.dma("gpsimd",
                          bass.AP(dn_d, s0 * D + dc * 128,
                                  [[D, 128], [128 * D, NT8], [1, 128]]),
                          bass.AP(dnT_sb, 0, [[NT8 * 128, 128], [128, NT8], [1, 128]]),
                          "LG")

        for t8 in range(NT8):
            xb = t8 % 2
            r0 = s0 + t8 * 128
            E.wait("sync", "LG", lg_dn)
            E.wait("sync", "VE", ve_ln2[xb])
            lsem = "DA" if xb == 0 else "DB"
            ld_dn = E.dma("sync", dnrow[xb][:], dn_d[r0:r0 + 128, :], lsem)
            ld_x2 = E.dma("sync", x2row[xb][:], x2_d[r0:r0 + 128, :], lsem)
            E.wait("vector", lsem, ld_x2)
            if y_st:
                E.wait("vector", "LD", y_st[-1])
            E.op("vector", lambda v, xb=xb, t8=t8: v.scalar_tensor_tensor(
                dnrow[xb][:], dnrow[xb][:], dqh8[:, t8:t8 + 1], x2row[xb][:],
                OP.mult, OP.add))
            E.op("vector", lambda v: v.drain())
            E.op("vector", lambda v, xb=xb: v.reduce_sum(lntf[:, 0:1],
                                                         dnrow[xb][:],
                                                         axis=AX.X))
            E.op("vector", lambda v: v.drain())
            E.op("vector", lambda v: v.tensor_scalar(lntf[:, 0:1], lntf[:, 0:1],
                                                     1.0 / D, None, OP.mult))
            E.op("vector", lambda v: v.drain())
            E.op("vector", lambda v, xb=xb: v.tensor_scalar(
                x2row[xb][:], dnrow[xb][:], lntf[:, 0:1], None, OP.subtract))
            ve_d2 = E.op("vector", lambda v: v.drain(), inc="VE")
            E.wait("scalar", "VE", ve_d2)
            E.op("scalar", lambda s, xb=xb: s.activation(dsqf16[:],
                                                         x2row[xb][:],
                                                         AF.Square))
            sc_q2 = E.op("scalar", lambda s: s.drain(), inc="SC")
            E.wait("vector", "SC", sc_q2)
            E.op("vector", lambda v: v.reduce_sum(lntf[:, 3:4], dsqf16[:],
                                                  axis=AX.X))
            E.op("vector", lambda v: v.drain())
            E.op("vector", lambda v: v.tensor_scalar(
                lntf[:, 3:4], lntf[:, 3:4], 1.0 / D, EPS, OP.mult, OP.add))
            E.op("vector", lambda v: v.drain())
            E.op("vector", lambda v: v.reciprocal(lntf[:, 4:5],
                                                   lntf[:, 3:4]))
            ve_r2 = E.op("vector", lambda v: v.drain(), inc="VE")
            E.wait("scalar", "VE", ve_r2)
            E.op("scalar", lambda s: s.activation(lntf[:, 5:6],
                                                  lntf[:, 4:5], AF.Sqrt))
            sc_r2 = E.op("scalar", lambda s: s.drain(), inc="SC")
            E.wait("vector", "SC", sc_r2)
            E.op("vector", lambda v, xb=xb: v.tensor_scalar(
                yout_sb[:], x2row[xb][:], lntf[:, 5:6], None, OP.mult))
            ve_y = E.op("vector", lambda v: v.drain(), inc="VE")
            ve_ln2[xb] = ve_y
            E.wait("sync", "VE", ve_y)
            y_st.append(E.dma("sync", y[r0:r0 + 128, :], yout_sb[:], "LD"))

    E.wait("sync", "LD", E.cnt["LD"])
    E.wait("gpsimd", "LG", E.cnt["LG"])
    E.replay()
    fes.close()
    es.close()
    return nc


_PERM = np.concatenate([np.arange(0, 128, 2), np.arange(1, 128, 2)])


def _rope_tables():
    inv = (1.0 / 10000.0 ** (np.arange(0, HD, 2, dtype=np.float64) / HD)
           ).astype(np.float32)
    t = np.arange(S, dtype=np.float32)
    freqs = (t[:, None] * inv[None, :]).astype(np.float32)
    cos = np.cos(freqs.astype(np.float64)).astype(np.float32).T
    sin = np.sin(freqs.astype(np.float64)).astype(np.float32).T
    return (np.ascontiguousarray(np.vstack([cos, cos])),
            np.ascontiguousarray(np.vstack([sin, sin])))


def _permute_cols(w, nheads):
    hd = 128
    idx = np.concatenate([h * hd + _PERM for h in range(nheads)])
    return np.ascontiguousarray(w[:, idx])


def kernel(L0_attn_hidden_states, L0_attn_Wq, L0_attn_Wk, L0_attn_Wv, L0_attn_Wo,
           L0_ffn_w_gate, L0_ffn_w_up, L0_ffn_w_down):
    x = np.ascontiguousarray(np.asarray(L0_attn_hidden_states, dtype=np.float32))
    wq_h = _permute_cols(np.asarray(L0_attn_Wq, dtype=np.float32), NH)
    wk_h = _permute_cols(np.asarray(L0_attn_Wk, dtype=np.float32), NKV)
    wv_h = np.ascontiguousarray(np.asarray(L0_attn_Wv, dtype=np.float32))
    wo_h = np.ascontiguousarray(np.asarray(L0_attn_Wo, dtype=np.float32))
    wgT_h = np.ascontiguousarray(np.asarray(L0_ffn_w_gate, dtype=np.float32).T)
    wuT_h = np.ascontiguousarray(np.asarray(L0_ffn_w_up, dtype=np.float32).T)
    wdT_h = np.ascontiguousarray(np.asarray(L0_ffn_w_down, dtype=np.float32).T)
    cos, sin = _rope_tables()
    id16 = np.eye(128, dtype=BF)
    id32 = np.eye(128, dtype=np.float32)

    onesk = np.zeros((64, 128), dtype=np.float32)
    onesk[0, :] = 1.0
    nc = build()
    in_maps = []
    for c in range(8):
        b, hf = c // 2, c % 2
        xb = x[b]
        xbT = np.ascontiguousarray(xb.T)
        in_maps.append({
            "xT": xbT,
            "xoT": np.ascontiguousarray(xbT[:, hf * HALF:(hf + 1) * HALF]),
            "x_tok": np.ascontiguousarray(xb[hf * HALF:(hf + 1) * HALF, :]),
            "wq": wq_h, "wk": wk_h, "wv": wv_h, "wo": wo_h,
            "wgT": wgT_h, "wuT": wuT_h, "wdT": wdT_h,
            "cosK": cos, "sinK": sin,
            "cosQ": np.ascontiguousarray(cos[:, hf * HALF:(hf + 1) * HALF]),
            "sinQ": np.ascontiguousarray(sin[:, hf * HALF:(hf + 1) * HALF]),
            "ident16": id16, "ident32": id32,
            "ones_in": np.ones((128, 1), dtype=BF),
            "nb_in": np.broadcast_to(np.array([-0.5, 0.5], dtype=np.float32),
                                     (128, 2)).copy(),
            "onesk_in": onesk,
            "z64_in": np.zeros((64, 8), dtype=np.float32),
        })
    res = run_bass_kernel_spmd(nc, in_maps, core_ids=list(range(8)))
    out = np.empty((B, S, D), dtype=np.float32)
    for c in range(8):
        b, hf = c // 2, c % 2
        out[b, hf * HALF:(hf + 1) * HALF, :] = res.results[c]["y"]
    return out
